# version 03 (final)
# baseline (speedup 1.0000x reference)
import math
import sys
import numpy as np

import concourse.bass as bass
import concourse.mybir as mybir
from concourse.bass_utils import run_bass_kernel_spmd

F32 = mybir.dt.float32
BF16 = mybir.dt.bfloat16
I32 = mybir.dt.int32
AF = mybir.ActivationFunctionType
OP = mybir.AluOpType
AX = mybir.AxisListType

D = 1024
NH = 16
DH = 64
FCN = 8
H2 = 2048
MC = 64
NE = 8
EH = 1024
C = 128
BIGNEG = -30000.0
MAGIC_RSQRT = 0x5F3759DF
MAGIC_RECIP = 0x7EF311C3


class _TeMock:
    def __init__(self):
        self.kind = None

    def matmul(self, o, l, r, **kw):
        try:
            b = l.base_partition()
            p = l.partition_size()
        except Exception:
            return self
        half = "hi" if b >= 64 else ("lo" if b + p <= 64 else "full")
        mode = "t" if kw.get("is_transpose") else "m"
        self.kind = (mode, half)
        return self

    def transpose(self, o, in_, ident, **kw):
        self.matmul(o, in_, ident)
        if self.kind is not None:
            self.kind = ("t", self.kind[1])
        return self

    def then_inc(self, *a):
        return self

    def __getattr__(self, name):
        def f(*a, **k):
            return self
        return f


class _VeMock:
    def __init__(self):
        self.reads = []
        self.writes = []

    def _rng(self, ap):
        try:
            W = ap.tensor.shape[1] if len(ap.tensor.shape) > 1 else 1
            b = ap.base_partition()
            p = ap.partition_size()
            span = 1
            for st, ct in list(ap.ap)[1:]:
                span += abs(st) * (ct - 1)
            c0 = ap.offset - b * W
            return (ap.tensor.name, b, b + p, c0, c0 + span)
        except Exception:
            return None

    def _add(self, args, kwargs):
        aps = []
        for a in args:
            r = self._rng(a) if hasattr(a, "offset") else None
            if r is not None:
                aps.append(r)
        for k, v in kwargs.items():
            r = self._rng(v) if hasattr(v, "offset") else None
            if r is not None:
                if k == "accum_out":
                    self.writes.append(r)
                else:
                    aps.append(r)
        if aps:
            self.writes.append(aps[0])
            self.reads.extend(aps[1:])

    def then_inc(self, *a):
        return self

    def __getattr__(self, name):
        def f(*a, **k):
            if name not in ("nop", "wait_ge"):
                self._add(a, k)
            return self
        return f


def _rng_overlap(a, b):
    return (a[0] == b[0] and a[1] < b[2] and b[1] < a[2]
            and a[3] < b[4] and b[3] < a[4])


def _ve_hazard(prev_writes, reads, writes):
    for w in prev_writes:
        for r in reads:
            if _rng_overlap(w, r):
                return True
        for r in writes:
            if _rng_overlap(w, r):
                return True
    return False


def _te_safe(a, b):
    if a is None or b is None:
        return True
    if (a[0] == "t" and a[1] != "full") or (b[0] == "t" and b[1] != "full"):
        return False
    if a == b:
        return True
    if a[1] == "full" and b[1] == "full":
        return True
    if a[0] == "m" and b[0] == "m" and "full" in (a[1], b[1]):
        return True
    return False


class Sch:
    ENGS = ("te", "ve", "se", "ge", "sy")

    def __init__(self):
        self.recs = []
        self.cnt = {e: 0 for e in self.ENGS}
        self.dcnt = {e: 0 for e in self.ENGS}

    def _waits(self, deps, own):
        m = {}
        for d in deps:
            if d is None:
                continue
            s, v = d
            if s == "e_" + own:
                continue
            if v > m.get(s, -1):
                m[s] = v
        return sorted(m.items())

    def op(self, eng, fn, deps=(), tag=False):
        if getattr(self, "dump_tok", None) is not None:
            deps = list(deps) + [self.dump_tok]
        rec = [eng, fn, self._waits(deps, eng), None, sys._getframe(2).f_lineno]
        tok = None
        if tag:
            self.cnt[eng] += 1
            rec[3] = ("e_" + eng, 1)
            tok = ("e_" + eng, self.cnt[eng])
        self.recs.append(rec)
        return tok

    def dma(self, eng, fn, deps=()):
        deps = list(deps) + [("d_" + eng, self.dcnt[eng])]
        self.dcnt[eng] += 16
        self.recs.append([eng, fn, self._waits(deps, eng), ("d_" + eng, 16),
                          sys._getframe(2).f_lineno])
        return ("d_" + eng, self.dcnt[eng])

    def replay(self, nc, block, sems, limit=None):
        recs = self.recs if limit is None else self.recs[:limit]
        halves = []
        veacc = []
        for r in recs:
            hh = None
            va = None
            if r[0] == "te":
                m = _TeMock()
                try:
                    r[1](m)
                except Exception:
                    pass
                hh = m.kind
            elif r[0] == "ve":
                m = _VeMock()
                try:
                    r[1](m)
                except Exception:
                    pass
                va = (m.reads, m.writes)
            halves.append(hh)
            veacc.append(va)
        def mk(eng_name):
            def f(h):
                seen = {}
                prev_half = None
                prev_w = None
                for ri, (eng, fn, waits, inc, _ln) in enumerate(recs):
                    if eng != eng_name:
                        continue
                    for s, v in waits:
                        if seen.get(s, -1) >= v:
                            continue
                        h.wait_ge(sems[s], v)
                        seen[s] = v
                    if eng_name == "te":
                        hh = halves[ri]
                        if hh is not None:
                            if not _te_safe(prev_half, hh):
                                h.nop()
                                h.nop()
                                h.nop()
                            prev_half = hh
                    elif eng_name == "ve":
                        va = veacc[ri]
                        if va is not None:
                            if prev_w and _ve_hazard(prev_w, va[0], va[1]):
                                h.tensor_copy(self.vspc[:, 1:2], self.vspc[:, 0:1])
                            prev_w = va[1]
                    inst = fn(h)
                    if inc is not None:
                        inst.then_inc(sems[inc[0]], inc[1])
            return f
        block.tensor(mk("te"))
        block.vector(mk("ve"))
        block.scalar(mk("se"))
        block.gpsimd(mk("ge"))
        block.sync(mk("sy"))


def host_consts():
    c = {}
    tu = np.triu(np.ones((128, 128), np.float32))
    c["cTU"] = tu
    c["cTUBIG"] = (BIGNEG * tu).astype(np.float32)
    c["cTLBIG"] = (BIGNEG * np.tril(np.ones((128, 128), np.float32), -1)).astype(np.float32)
    c["cNEG1"] = -np.ones((128, 128), np.float32)
    c["cONES"] = np.ones((128, 128), np.float32)
    c["cI128"] = np.eye(128, dtype=np.float32)
    ind = np.zeros((16, 2048), np.float32)
    for h in range(16):
        ind[h, 128 * h:128 * (h + 1)] = 1.0
    c["cIND128"] = ind
    indW = np.zeros((16, 2048), np.float32)
    indU = np.zeros((16, 2048), np.float32)
    for h in range(16):
        indW[h, 128 * h:128 * h + 64] = 1.0
        indU[h, 128 * h + 64:128 * h + 128] = 1.0
    c["cINDW"] = indW
    c["cINDU"] = indU
    ind64 = np.zeros((16, 1024), np.float32)
    for h in range(16):
        ind64[h, 64 * h:64 * (h + 1)] = 1.0
    c["cIND64"] = ind64
    ind64p = np.zeros((16, 512), np.float32)
    for h in range(16):
        ind64p[h, 64 * (h // 2):64 * (h // 2) + 64] = 1.0
    c["cIND64P"] = ind64p
    ph = np.zeros((16, 128), np.float32)
    for h in range(16):
        ph[h, 64 * (h % 2):64 * (h % 2) + 64] = 1.0
    c["cPAIRH"] = ph
    indfc = np.zeros((128, 8 * 128), np.float32)
    for j in range(8):
        for h in range(16):
            if h // 2 == j:
                sl = slice(128 * j + 64 * (h % 2), 128 * j + 64 * (h % 2) + 64)
                indfc[h, sl] = 1.0
                indfc[32 + h, sl] = 1.0
    c["cINDFC"] = indfc
    blk = np.zeros((128, 8 * 16), np.float32)
    for j in range(8):
        for p in range(128):
            blk[p, 16 * j + 2 * j + p // 64] = 1.0
    c["cBLK"] = blk
    c["cONESROW"] = np.ones((1, 512), np.float32)
    c["cONESCOL"] = np.ones((128, 1), np.float32)
    c["cONES16x64"] = np.ones((16, 64), np.float32)
    return c


def host_prep(inputs):
    f32 = np.float32
    ct = np.ascontiguousarray
    sh = dict(host_consts())
    sh["wq"] = ct(np.asarray(inputs["L0_attn_Wq"], f32).T)
    sh["wk"] = ct(np.asarray(inputs["L0_attn_Wk"], f32).T)
    sh["wv"] = ct(np.asarray(inputs["L0_attn_Wv"], f32).T)
    sh["wg"] = ct(np.asarray(inputs["L0_attn_Wg"], f32).T)
    sh["wo"] = ct(np.asarray(inputs["L0_attn_Wo"], f32).T)
    sh["wba"] = ct(np.concatenate([np.asarray(inputs["L0_attn_Wb"], f32).T,
                                   np.asarray(inputs["L0_attn_Wa"], f32).T], axis=1))
    taps = []
    for nm in ("L0_attn_conv_q", "L0_attn_conv_k", "L0_attn_conv_v"):
        w = np.asarray(inputs[nm], f32).reshape(8, 128, 4)
        taps.append(np.transpose(w, (1, 0, 2)).reshape(128, 32))
    sh["ctaps"] = ct(np.concatenate(taps, axis=1))
    sh["calog"] = ct(np.tile(np.asarray(inputs["L0_attn_A_log"], f32)[None, :], (128, 4)))
    sh["cdtb"] = ct(np.tile(np.asarray(inputs["L0_attn_dt_bias"], f32)[None, :], (128, 4)))
    sh["cnw"] = ct(np.tile(np.asarray(inputs["L0_attn_norm_w"], f32), 2)[:, None])
    sh["wmod"] = np.asarray(inputs["L0_wmod"], f32)
    sh["wenc"] = np.asarray(inputs["L0_ffn_W_enc"], f32)
    sh["benc"] = ct(np.asarray(inputs["L0_ffn_b_enc"], f32).reshape(16, 128).T)
    sh["ws2c"] = np.asarray(inputs["L0_ffn_W_s2c"], f32)
    sh["bs2c_row"] = ct(np.asarray(inputs["L0_ffn_b_s2c"], f32)[None, :])
    sh["wr"] = np.asarray(inputs["L0_ffn_Wr"], f32)
    sh["br_row"] = ct(np.asarray(inputs["L0_ffn_br"], f32)[None, :])
    w1 = np.asarray(inputs["L0_ffn_W1"], f32)
    w1p = np.zeros((128, 4 * 1024), f32)
    for e in range(8):
        w1p[(e % 2) * 64:(e % 2) * 64 + 64, (e // 2) * 1024:(e // 2) * 1024 + 1024] = w1[e]
    sh["w1p"] = w1p
    b1 = np.asarray(inputs["L0_ffn_b1"], f32)
    b1p = np.zeros((128, 64), f32)
    for e in range(8):
        for mo in range(8):
            b1p[:, e * 8 + mo] = b1[e, 128 * mo:128 * (mo + 1)]
    sh["b1p"] = b1p
    w2 = np.asarray(inputs["L0_ffn_W2"], f32)
    w2p = np.zeros((128, 8 * 8 * 64), f32)
    for e in range(8):
        for ko in range(8):
            w2p[:, (e * 8 + ko) * 64:(e * 8 + ko) * 64 + 64] = w2[e, 128 * ko:128 * (ko + 1), :]
    sh["w2p"] = w2p
    sh["b2_row"] = ct(np.asarray(inputs["L0_ffn_b2"], f32).reshape(1, 512))
    sh["wc2s"] = np.asarray(inputs["L0_ffn_W_c2s"], f32)
    sh["bc2s"] = ct(np.asarray(inputs["L0_ffn_b_c2s"], f32).reshape(16, 128).T)
    sh["wdec"] = np.asarray(inputs["L0_ffn_W_dec"], f32)
    sh["bdec_row"] = ct(np.asarray(inputs["L0_ffn_b_dec"], f32)[None, :])
    sh["lng_row"] = ct(np.asarray(inputs["L0_ffn_ln_g"], f32)[None, :])
    sh["lnb_row"] = ct(np.asarray(inputs["L0_ffn_ln_b"], f32)[None, :])

    x = np.asarray(inputs["L0_attn_hidden"], f32)
    cc = np.asarray(inputs["L0_c"], f32)
    per_core = []
    for i in range(x.shape[0]):
        d = {"x": ct(x[i]),
             "ccol": ct(cc[i].reshape(8, 128).T)}
        per_core.append(d)
    return sh, per_core


OC = 256
CPO = OC // C
BS = 64
BIG_W_COLS = 8 * D


def build_program(T=4096, dbg=(), limit=None):
    import contextlib
    NOC = T // OC
    nc = bass.Bass("TRN2", target_bir_lowering=False, debug=False, detect_race_conditions=False)
    es = contextlib.ExitStack()

    dr = {}
    def din(name, shape):
        dr[name] = nc.dram_tensor(name, list(shape), F32, kind="ExternalInput").ap()
        return dr[name]
    for nm, shp in [("x", (T, D)), ("ccol", (128, 8)),
                    ("wq", (D, D)), ("wk", (D, D)), ("wv", (D, D)), ("wg", (D, D)),
                    ("wo", (D, D)), ("wba", (D, 32)), ("wmod", (D, 6 * D)),
                    ("ctaps", (128, 96)), ("calog", (128, 64)), ("cdtb", (128, 64)),
                    ("cnw", (128, 1)),
                    ("cTU", (128, 128)), ("cTUBIG", (128, 128)), ("cTLBIG", (128, 128)),
                    ("cNEG1", (128, 128)), ("cONES", (128, 128)), ("cI128", (128, 128)),
                    ("cIND128", (16, 2048)), ("cINDW", (16, 2048)), ("cINDU", (16, 2048)),
                    ("cIND64", (16, 1024)), ("cIND64P", (16, 512)), ("cPAIRH", (16, 128)),
                    ("cINDFC", (128, 1024)), ("cBLK", (128, 128)),
                    ("cONESROW", (1, 512)), ("cONESCOL", (128, 1)), ("cONES16x64", (16, 64)),
                    ("wenc", (D, H2)), ("benc", (128, 16)), ("ws2c", (H2, MC)),
                    ("bs2c_row", (1, MC)), ("wr", (MC, NE)), ("br_row", (1, NE)),
                    ("w1p", (128, 4096)), ("b1p", (128, 64)), ("w2p", (128, 4096)),
                    ("b2_row", (1, 512)), ("wc2s", (MC, H2)), ("bc2s", (128, 16)),
                    ("wdec", (H2, D)), ("bdec_row", (1, D)),
                    ("lng_row", (1, D)), ("lnb_row", (1, D))]:
        din(nm, shp)
    xmid_d = nc.dram_tensor("xmid", [T, D], F32, kind="Internal").ap()
    out_d = nc.dram_tensor("out", [T, D], F32, kind="ExternalOutput").ap()
    dbg_d = {}
    def dbgout(name, shape):
        dbg_d[name] = nc.dram_tensor("dbg_" + name, list(shape), F32, kind="ExternalOutput").ap()
        return dbg_d[name]
    def dump(name, ap, idx, nidx, deps):
        if not dbg:
            return None
        shp = [nidx] + list(ap.shape)
        if name not in dbg_d:
            dbgout(name, shp)
        t = DMA("ge", lambda h, o=dbg_d[name][idx], a=ap: h.dma_start(o, a), deps=deps)
        sch.dump_tok = t
        return t

    cb = {}
    for nm in ("cTU", "cTUBIG", "cTLBIG", "cNEG1", "cONES", "cI128", "cIND128",
               "cINDW", "cINDU", "cIND64", "cIND64P", "cPAIRH", "cINDFC", "cBLK",
               "cONESROW", "cONESCOL", "cONES16x64"):
        cb[nm] = nc.alloc_sbuf_tensor("s_" + nm, list(dr[nm].shape), BF16).ap()
    cI128f = nc.alloc_sbuf_tensor("s_cI128f", [128, 128], F32).ap()
    cf = {}
    for nm in ("ctaps", "calog", "cdtb", "cnw", "ccol"):
        cf[nm] = nc.alloc_sbuf_tensor("s_" + nm, list(dr[nm].shape), F32).ap()

    x_sb = nc.alloc_sbuf_tensor("x_sb", [128, (OC // 128) * D], F32).ap()
    S_sb = nc.alloc_sbuf_tensor("S_sb", [128, 512], F32).ap()
    Sscr = nc.alloc_sbuf_tensor("Sscr", [128, 512], F32).ap()
    bg_sb = nc.alloc_sbuf_tensor("bg_sb", [128, 2 * OC // 8], F32).ap()
    g_sb = nc.alloc_sbuf_tensor("g_sb", [128, OC // 8], F32).ap()
    spb_sb = nc.alloc_sbuf_tensor("spb_sb", [128, OC // 8], F32).ap()
    gsc_sb = nc.alloc_sbuf_tensor("gsc_sb", [128, OC // 8], F32).ap()
    gsc2_sb = nc.alloc_sbuf_tensor("gsc2_sb", [128, OC // 8], F32).ap()
    b_tm = nc.alloc_sbuf_tensor("b_tm", [128, 16], F32).ap()
    stat = nc.alloc_sbuf_tensor("stat", [128, 48], F32).ap()
    rowsA = nc.alloc_sbuf_tensor("rowsA", [128, 512], F32).ap()
    rowsR = nc.alloc_sbuf_tensor("rowsR", [128, OC], F32).ap()
    rowsN = nc.alloc_sbuf_tensor("rowsN", [128, 3 * OC], F32).ap()
    modc = nc.alloc_sbuf_tensor("modc", [128, 64], F32).ap()

    expA = nc.alloc_sbuf_tensor("expA", [128, OC // 8], F32).ap()
    csil_f = nc.alloc_sbuf_tensor("csil_f", [128, 8], F32).ap()
    lnc8 = nc.alloc_sbuf_tensor("lnc8", [128, 1], F32).ap()
    eo_f = nc.alloc_sbuf_tensor("eo_f", [128, OC], F32).ap()
    yg_f = nc.alloc_sbuf_tensor("yg_f", [128, OC], F32).ap()
    gsel_f = nc.alloc_sbuf_tensor("gsel_f", [128, 16], F32).ap()
    tk_f = nc.alloc_sbuf_tensor("tk_f", [128, 24], F32).ap()
    benc_h = nc.alloc_sbuf_tensor("benc_h", [128, 16], F32).ap()
    bc2s_h = nc.alloc_sbuf_tensor("bc2s_h", [128, 16], F32).ap()
    b1p_s = nc.alloc_sbuf_tensor("b1p_s", [128, 64], F32).ap()

    rbA = nc.alloc_sbuf_tensor("rbA", [128, 768], BF16).ap()
    rbB = nc.alloc_sbuf_tensor("rbB", [128, 512], BF16).ap()
    csil_b = nc.alloc_sbuf_tensor("csil_b", [128, 8], BF16).ap()
    carry = nc.alloc_sbuf_tensor("carry", [128, 96], BF16).ap()
    g_bf = nc.alloc_sbuf_tensor("g_bf", [128, OC // 8], BF16).ap()
    ngbf = nc.alloc_sbuf_tensor("ngbf", [128, OC // 8], BF16).ap()
    rowslot = nc.alloc_sbuf_tensor("rowslot", [1, 1024], BF16).ap()

    Abc = nc.alloc_sbuf_tensor("Abc", [128, 1024], BF16).ap()
    Bbc = nc.alloc_sbuf_tensor("Bbc", [128, 1024], BF16).ap()
    A2bc = nc.alloc_sbuf_tensor("A2bc", [128, 1024], BF16).ap()
    B2bc = nc.alloc_sbuf_tensor("B2bc", [128, 1024], BF16).ap()
    G2bc = nc.alloc_sbuf_tensor("G2bc", [128, 1024], BF16).ap()
    BB2bc = nc.alloc_sbuf_tensor("BB2bc", [128, 1024], BF16).ap()
    gabc = nc.alloc_sbuf_tensor("gabc", [128, 1024], BF16).ap()

    ABF = 72 * 1024
    arena = nc.alloc_sbuf_tensor("arena", [128, ABF], BF16).ap()
    _aoff = [0]
    def take(cols):
        a = arena[:, _aoff[0]:_aoff[0] + cols]
        _aoff[0] += cols
        assert _aoff[0] <= ABF, _aoff[0]
        return a
    wq_s = take(BIG_W_COLS); wk_s = take(BIG_W_COLS)
    wvg_s = take(BIG_W_COLS)
    wo_s = take(BIG_W_COLS)
    wba_s = take(256)
    h1T = take(8 * OC)
    qsil = take(8 * OC)
    kl2n = take(8 * OC)
    vtm = take(8 * OC)
    og = take(8 * OC)
    o_fm = take(8 * OC)
    ktm_s = take(1024)
    E_s = take(2048)
    KKQ = take(2048)
    Ub = take(2048)
    L_s = take(2048)
    AT_s = take(2048)
    Y_s = take(2048)
    _mr0 = _aoff[0] - 2048
    Z_s = take(2048)
    nZ0 = take(2048)
    Ua = take(2048)
    modrow = arena[0:1, _mr0:_mr0 + 8192]
    Kdec_s = take(1024)
    JT_s = take(1024)
    Qgc = take(1024)
    GT_s = take(512)
    Sbf = take(512)
    sgs = take(OC)
    ext_s = take(OC + 8)
    sq_s = take(OC)
    cv_s = take(OC)
    h1tm = take(2048)
    ksil = og
    gm_s = Ub
    wst_s = arena[:, _aoff[0] - 6144:_aoff[0]] if False else arena[:, ABF - 6144:ABF]
    p1_high = _aoff[0]
    _aoff[0] = 0
    wenc_s = take(16 * D)
    wdec_s = take(16 * D)
    spk_s = take(16 * OC)
    smt_s = take(16 * OC)
    h2T = take(8 * OC)
    he_s = take(8 * OC)
    ws2c_s = take(1024)
    w1p_s = take(4096)
    w2p_s = take(4096)
    wc2s_s = take(2048)
    contp = take(OC)
    dscr = take(1024)
    gbc_s = take(OC)
    eobf = take(OC)
    wr_s = take(16)
    br_s = take(8)
    b2r_s = take(512)
    bs2cr = take(64)
    bdecr = take(1024)
    gselb = take(16)
    gselT_s = take(2048)
    p2scr = take(2048)
    _aoff[0] = max(_aoff[0], p1_high)

    Pb = [nc.alloc_psum_tensor("P%d" % i, [128, 512], F32).ap() for i in range(6)]
    PT0 = nc.alloc_psum_tensor("PT0", [128, 1024], BF16).ap()
    PT1 = nc.alloc_psum_tensor("PT1", [128, 512], F32).ap()

    sems = {}
    for s in ("e_te", "e_ve", "e_se", "e_ge", "e_sy", "d_ge", "d_sy"):
        sems[s] = es.enter_context(nc.semaphore(s))

    sch = Sch()
    sch.vspc = nc.alloc_sbuf_tensor("vspc", [128, 2], F32).ap()

    def OPt(eng, fn, deps=(), tag=False):
        return sch.op(eng, fn, deps, tag)

    def DMA(eng, fn, deps=()):
        return sch.dma(eng, fn, deps)

    def fmv(tile, t0, w):
        return tile.rearrange("p (j t) -> p j t", j=8)[:, :, t0:t0 + w]

    def emit_rsqrt(src_ap, out_ap, t1, t2, deps=(), n_iter=2, post_scale=None):
        OPt("se", lambda h, a=t1, b=src_ap: h.activation(a, b, AF.Ln), deps)
        t_sd = OPt("se", lambda h, a=t1: h.activation(a, a, AF.Exp, scale=-0.5), tag=True)
        dep0 = [t_sd]
        for it in range(n_iter):
            OPt("ve", lambda h, a=t2, b=t1: h.tensor_mul(a, b, b), deps=dep0)
            dep0 = ()
            OPt("ve", lambda h, a=t2, b=src_ap: h.tensor_mul(a, a, b))
            OPt("ve", lambda h, a=t2: h.tensor_scalar(a, a, -0.5, 1.5, OP.mult, OP.add))
            last = it == n_iter - 1
            dst = out_ap if (last and post_scale is None) else t1
            tok = OPt("ve", lambda h, a=dst, b=t1, c=t2: h.tensor_mul(a, b, c),
                      tag=(last and post_scale is None))
        if post_scale is not None:
            tok = OPt("ve", lambda h, a=out_ap, b=t1, s=post_scale:
                      h.tensor_scalar(a, b, s, None, OP.mult), tag=True)
        return tok

    ge_w = []
    for nm, dst in [("cTU", cb["cTU"]), ("cTUBIG", cb["cTUBIG"]), ("cTLBIG", cb["cTLBIG"]),
                    ("cNEG1", cb["cNEG1"]), ("cONES", cb["cONES"]), ("cI128", cb["cI128"]),
                    ("cIND128", cb["cIND128"]), ("cINDW", cb["cINDW"]), ("cINDU", cb["cINDU"]),
                    ("cIND64", cb["cIND64"]), ("cIND64P", cb["cIND64P"]), ("cPAIRH", cb["cPAIRH"]),
                    ("cINDFC", cb["cINDFC"]), ("cBLK", cb["cBLK"]), ("cONESROW", cb["cONESROW"]),
                    ("cONESCOL", cb["cONESCOL"]), ("cONES16x64", cb["cONES16x64"])]:
        ge_w.append(DMA("ge", lambda h, a=dst, b=dr[nm]: h.dma_start(a, b)))
    tok_cb = ge_w[-1]
    tok_cf = None
    for nm, dst in [("ctaps", cf["ctaps"]), ("calog", cf["calog"]), ("cdtb", cf["cdtb"]),
                    ("cnw", cf["cnw"]), ("ccol", cf["ccol"])]:
        tok_cf = DMA("sy", lambda h, a=dst, b=dr[nm]: h.dma_start(a, b))
    tok_i128f = DMA("sy", lambda h, a=cI128f, b=dr["cI128"]: h.dma_start(a, b))
    def wview(name):
        return dr[name].rearrange("(ko p) n -> p ko n", p=128)
    def w3(dst, n):
        return dst.rearrange("p (ko n) -> p ko n", n=n)
    tok_wq = DMA("ge", lambda h, a=w3(wq_s, 1024), b=wview("wq"): h.dma_start(a, b))
    tok_wk = DMA("ge", lambda h, a=w3(wk_s, 1024), b=wview("wk"): h.dma_start(a, b))
    tok_wo = DMA("ge", lambda h, a=w3(wo_s, 1024), b=wview("wo"): h.dma_start(a, b))
    tok_wba = DMA("ge", lambda h, a=w3(wba_s, 32), b=dr["wba"].rearrange("(ko p) n -> p ko n", p=128): h.dma_start(a, b))
    def xin_view(oc):
        return dr["x"][oc * OC:(oc + 1) * OC, :].rearrange("(i p) f -> p i f", p=128)
    x3 = x_sb.rearrange("p (i f) -> p i f", f=D)
    tok_x = DMA("sy", lambda h, a=x3, b=xin_view(0): h.dma_start(a, b), deps=[tok_cf])

    OPt("ve", lambda h: h.memset(sch.vspc, 0.0))
    tok_S0 = OPt("ve", lambda h: h.memset(S_sb, 0.0), tag=True)
    tok_ln8 = OPt("ve", lambda h: h.memset(lnc8, math.log(0.125)), tag=True)
    tok_car = OPt("ve", lambda h: h.memset(carry, 0.0), tag=True)

    tok_csil = OPt("se", lambda h: h.activation(csil_f, cf["ccol"], AF.Silu), deps=[tok_cf], tag=True)
    tok_csb = OPt("ve", lambda h: h.tensor_copy(csil_b, csil_f), deps=[tok_csil], tag=True)
    tok_expA = OPt("se", lambda h: h.activation(expA, cf["calog"][:, 0:OC // 8], AF.Exp), deps=[tok_cf], tag=True)
    tok_wst = None
    tok_mod_mm = None
    tok_modc = None
    for ko in range(8):
        dep = [tok_csb] + ([tok_modc] if tok_modc else [])
        tok_wst = DMA("ge", lambda h, a=wst_s, b=dr["wmod"][ko * 128:(ko + 1) * 128, :], ko=ko:
                      h.dma_start(a, b), deps=dep)
        for mo in range(48):
            tok_mod_mm = OPt("te", lambda h, o=Pb[5][:, mo:mo + 1], l=wst_s[:, mo * 128:(mo + 1) * 128],
                             r=csil_b[:, ko:ko + 1]:
                             h.matmul(o, l, r, start=True, stop=True),
                             deps=[tok_wst, tok_modc], tag=(mo == 47))
        if ko == 0:
            tok_modc = OPt("ve", lambda h: h.tensor_copy(modc[:, 0:48], Pb[5][:, 0:48]),
                           deps=[tok_mod_mm], tag=True)
        else:
            tok_modc = OPt("ve", lambda h: h.tensor_tensor(modc[:, 0:48], modc[:, 0:48], Pb[5][:, 0:48], OP.add),
                           deps=[tok_mod_mm], tag=True)
    tok_clamp = None
    for c0 in (16, 40):
        ga = modc[:, c0:c0 + 8]
        t1 = stat[:, 32:40]
        t2 = stat[:, 40:48]
        t_abs = OPt("se", lambda h, a=t1, b=ga: h.activation(a, b, AF.Abs),
                    deps=[tok_modc] + ([tok_clamp] if tok_clamp else []), tag=True)
        OPt("ve", lambda h, a=t1: h.tensor_scalar(a, a, 0.25, None, OP.max), deps=[tok_modc, t_abs])
        OPt("ve", lambda h, a=t2, b=ga: h.tensor_scalar(a, b, 0.0, None, OP.is_ge))
        OPt("ve", lambda h, a=t2: h.tensor_scalar(a, a, 2.0, -1.0, OP.mult, OP.add))
        tok_clamp = OPt("ve", lambda h, a=ga, b=t1, c=t2: h.tensor_mul(a, b, c), tag=True)
    OPt("ve", lambda h: h.tensor_scalar(modc[:, 48:56], modc[:, 8:16], 1.0, None, OP.add),
        deps=[tok_clamp])
    tok_pre = OPt("ve", lambda h: h.tensor_scalar(modc[:, 56:64], modc[:, 32:40], 1.0, None, OP.add),
                  tag=True)
    tok_mrow = None
    for r4 in range(16):
        t_mtp = None
        for j in range(4):
            t_mtp = OPt("te", lambda h, o=PT1[0:1, j * 128:(j + 1) * 128],
                        a=modc[:, r4 * 4 + j:r4 * 4 + j + 1]:
                        h.matmul(o, a, cI128f, start=True, stop=True, is_transpose=True),
                        deps=[tok_pre, tok_i128f, tok_mrow], tag=(j == 3))
        tok_mrow = OPt("ve", lambda h, a=modrow[0:1, r4 * 512:(r4 + 1) * 512],
                       b=PT1[0:1, 0:512]: h.tensor_copy(a, b), deps=[t_mtp], tag=True)
    tok_mT2 = tok_mrow
    if dbg:
        dump("csilD", csil_f, 0, 1, [tok_csil])
        dump("modcD", modc, 0, 1, [tok_pre])
        dump("mrowD", modrow, 0, 1, [tok_mT2])

    def emit_bcast(dst_tile, row0, deps, mul_into=False):
        mm_tok = None
        for k in range(8):
            ps = Pb[0] if k < 4 else Pb[1]
            mm_tok = OPt("te", lambda h, o=ps[:, 128 * (k % 4):128 * (k % 4) + 128],
                         l=cb["cONESROW"][0:1, 0:128],
                         r=modrow[0:1, (row0 + k) * 128:(row0 + k + 1) * 128]:
                         h.matmul(o, l, r, start=True, stop=True),
                         deps=[tok_mT2, tok_cb] + list(deps), tag=(k in (3, 7)))
        if mul_into:
            OPt("ve", lambda h: h.tensor_tensor(dst_tile[:, 0:512], dst_tile[:, 0:512], Pb[0], OP.mult),
                deps=[mm_tok])
            e2 = OPt("ve", lambda h: h.tensor_tensor(dst_tile[:, 512:1024], dst_tile[:, 512:1024],
                                                     Pb[1], OP.mult), deps=[mm_tok], tag=True)
        else:
            OPt("ve", lambda h: h.tensor_copy(dst_tile[:, 0:512], Pb[0]), deps=[mm_tok])
            e2 = OPt("ve", lambda h: h.tensor_copy(dst_tile[:, 512:1024], Pb[1]), deps=[mm_tok], tag=True)
        return e2, mm_tok

    tok_Abc, m1 = emit_bcast(Abc, 48, [tok_mT2])
    tok_Bbc, m2 = emit_bcast(Bbc, 0, [tok_Abc, m1])
    tok_A2bc, m3 = emit_bcast(A2bc, 56, [tok_Bbc, m2])
    tok_B2bc, m4 = emit_bcast(B2bc, 24, [tok_A2bc, m3])
    tok_gabc, m5 = emit_bcast(gabc, 16, [tok_B2bc, m4])
    tok_gbbc, m6 = emit_bcast(G2bc, 40, [tok_gabc, m5])
    tok_lng = DMA("ge", lambda h: h.dma_start(rowslot[0:1, 0:1024], dr["lng_row"]), deps=[m6, tok_gbbc])
    mm_tok = None
    for k in range(8):
        ps = Pb[0] if k < 4 else Pb[1]
        mm_tok = OPt("te", lambda h, o=ps[:, 128 * (k % 4):128 * (k % 4) + 128],
                     l=cb["cONESROW"][0:1, 0:128], r=rowslot[0:1, 128 * k:128 * k + 128],
                     st=True: h.matmul(o, l, r, start=st, stop=True),
                     deps=[tok_lng, tok_gbbc], tag=(k in (3, 7)))
    OPt("ve", lambda h: h.tensor_tensor(G2bc[:, 0:512], G2bc[:, 0:512], Pb[0], OP.mult), deps=[mm_tok])
    tok_G2 = OPt("ve", lambda h: h.tensor_tensor(G2bc[:, 512:1024], G2bc[:, 512:1024], Pb[1], OP.mult), tag=True)
    tok_gbb2, m7 = emit_bcast(BB2bc, 40, [tok_G2, mm_tok])
    tok_lnb = DMA("ge", lambda h: h.dma_start(rowslot[0:1, 0:1024], dr["lnb_row"]), deps=[m7, tok_gbb2])
    mm_tok = None
    for k in range(8):
        ps = Pb[0] if k < 4 else Pb[1]
        mm_tok = OPt("te", lambda h, o=ps[:, 128 * (k % 4):128 * (k % 4) + 128],
                     l=cb["cONESROW"][0:1, 0:128], r=rowslot[0:1, 128 * k:128 * k + 128],
                     st=True: h.matmul(o, l, r, start=st, stop=True),
                     deps=[tok_lnb, tok_gbb2], tag=(k in (3, 7)))
    OPt("ve", lambda h: h.tensor_tensor(BB2bc[:, 0:512], BB2bc[:, 0:512], Pb[0], OP.mult), deps=[mm_tok])
    tok_BB2 = OPt("ve", lambda h: h.tensor_tensor(BB2bc[:, 512:1024], BB2bc[:, 512:1024], Pb[1], OP.mult), tag=True)
    tok_rowslot_free = mm_tok
    if dbg:
        dump("AbcD", Abc, 0, 1, [tok_Abc])
        dump("BbcD", Bbc, 0, 1, [tok_Bbc])
        dump("G2D", G2bc, 0, 1, [tok_G2])
        dump("BB2D", BB2bc, 0, 1, [tok_BB2])
    OPt("ve", lambda h: h.tensor_scalar(wo_s, wo_s, cf["cnw"], None, OP.mult), deps=[tok_wo, tok_cf])
    tok_wop = None
    for ko in range(8):
        tok_wop = OPt("ve", lambda h, a=wo_s[:, ko * 1024:(ko + 1) * 1024]:
                      h.tensor_tensor(a, a, gabc, OP.mult), deps=[tok_gabc], tag=(ko == 7))

    NT = OC // 128
    tok_store = None
    tok_resid = None
    tok_wv_free = tok_BB2
    tok_F_prev = None
    tok_gate_mm = None
    t_sil = {}
    INV_D = 1.0 / float(D)

    for oc in range(NOC):
        oc_deps = [d for d in (tok_resid, tok_store) if d is not None]
        if oc > 0:
            tok_x = DMA("sy", lambda h, b=xin_view(oc): h.dma_start(x3, b), deps=oc_deps)
        t_sq = []
        t_sm = []
        for i in range(NT):
            xi = x_sb[:, i * 1024:(i + 1) * 1024]
            t_sq.append(OPt("se", lambda h, a=og[:, 0:1024], b=xi, acc=stat[:, 4 + i:5 + i]:
                            h.activation(a, b, AF.Square, accum_out=acc),
                            deps=[tok_x, tok_wop], tag=True))
            t_sm.append(OPt("ve", lambda h, a=stat[:, i:i + 1], b=xi:
                            h.reduce_sum(a, b, AX.X), deps=[tok_x], tag=True))
        OPt("ve", lambda h: h.tensor_scalar(stat[:, 8:8 + NT], stat[:, 0:NT], INV_D, None, OP.mult),
            deps=t_sq + t_sm)
        OPt("ve", lambda h: h.tensor_mul(stat[:, 12:12 + NT], stat[:, 8:8 + NT], stat[:, 8:8 + NT]))
        OPt("ve", lambda h: h.tensor_scalar(stat[:, 16:16 + NT], stat[:, 4:4 + NT], INV_D, None, OP.mult))
        OPt("ve", lambda h: h.tensor_sub(stat[:, 16:16 + NT], stat[:, 16:16 + NT], stat[:, 12:12 + NT]))
        t_eps_2 = OPt("ve", lambda h: h.tensor_scalar(stat[:, 16:16 + NT], stat[:, 16:16 + NT],
                                                      1e-5, None, OP.add), tag=True)
        tok_rstd = emit_rsqrt(stat[:, 16:16 + NT], stat[:, 20:20 + NT],
                              stat[:, 24:24 + NT], stat[:, 28:28 + NT], deps=[t_eps_2])
        t_h1T = []
        t_pt_free = None
        for i in range(NT):
            xi = x_sb[:, i * 1024:(i + 1) * 1024]
            OPt("ve", lambda h, a=h1tm[:, 0:1024], b=xi, m=stat[:, 8 + i:9 + i], r=stat[:, 20 + i:21 + i]:
                h.tensor_scalar(a, b, m, r, OP.subtract, OP.mult), deps=[tok_rstd, t_sq[i]])
            OPt("ve", lambda h, a=h1tm[:, 1024:2048], b=h1tm[:, 0:1024]:
                h.tensor_tensor(a, b, Abc, OP.mult), deps=[tok_Abc])
            t_h1b = OPt("ve", lambda h, a=h1tm[:, 0:1024], b=h1tm[:, 1024:2048]:
                        h.tensor_tensor(a, b, Bbc, OP.add), deps=[tok_Bbc], tag=True)
            t_tp = None
            for j in range(8):
                t_tp = OPt("te", lambda h, o=PT0[:, j * 128:(j + 1) * 128],
                           a=h1tm[:, j * 128:(j + 1) * 128], st=True:
                           h.matmul(o, a, cb["cI128"], start=st, stop=True, is_transpose=True),
                           deps=[t_h1b, t_pt_free, tok_cb], tag=(j == 7))
            t_pt_free = OPt("ve", lambda h,
                            o=h1T.rearrange("p (j t) -> p j t", j=8)[:, :, i * 128:(i + 1) * 128],
                            a=PT0.rearrange("p (j t) -> p j t", j=8):
                            h.tensor_copy(o, a), deps=[t_tp], tag=True)
            t_h1T.append(t_pt_free)
        tok_h1T = t_h1T[-1]

        t_ev = [None, None]
        _dump_cv = [False]
        def emit_proj_conv(tsr, w_s, out_silu, dep_w, extra_mm_deps=()):
            toks = []
            t_mm = None
            for mo in range(8):
                for ko in range(8):
                    t_mm = OPt("te", lambda h, o=Pb[mo % 2][:, 0:OC],
                               l=w_s[:, ko * 1024 + mo * 128: ko * 1024 + (mo + 1) * 128],
                               r=h1T[:, ko * OC:(ko + 1) * OC], st=(ko == 0), sp=(ko == 7):
                               h.matmul(o, l, r, start=st, stop=sp),
                               deps=[tok_h1T, dep_w, t_ev[mo % 2]] + list(extra_mm_deps),
                               tag=(ko == 7))
                prev_cons = toks[-1] if toks else None
                OPt("ve", lambda h, a=ext_s[:, 0:3], b=carry[:, tsr * 24 + mo * 3: tsr * 24 + (mo + 1) * 3]:
                    h.tensor_copy(a, b), deps=[tok_car, prev_cons])
                t_ev[mo % 2] = OPt("ve", lambda h, a=ext_s[:, 3:3 + OC], b=Pb[mo % 2][:, 0:OC]:
                                   h.tensor_copy(a, b), deps=[t_mm], tag=True)
                if dbg and tsr == 0 and mo == 0:
                    dump("extq0", ext_s, oc, NOC, [t_ev[mo % 2]])
                    _dump_cv[0] = True
                tp = lambda j: cf["ctaps"][:, tsr * 32 + mo * 4 + j: tsr * 32 + mo * 4 + j + 1]
                OPt("ve", lambda h, a=sq_s, b=ext_s[:, 3:3 + OC], t=tp(3):
                    h.tensor_scalar(a, b, t, None, OP.mult))
                OPt("ve", lambda h, a=cv_s, b=ext_s[:, 2:2 + OC], t=tp(2):
                    h.scalar_tensor_tensor(a, b, t, sq_s, OP.mult, OP.add))
                OPt("ve", lambda h, a=sq_s, b=ext_s[:, 1:1 + OC], t=tp(1):
                    h.scalar_tensor_tensor(a, b, t, cv_s, OP.mult, OP.add))
                t_cv = OPt("ve", lambda h, a=cv_s, b=ext_s[:, 0:OC], t=tp(0):
                           h.scalar_tensor_tensor(a, b, t, sq_s, OP.mult, OP.add), tag=True)
                OPt("ve", lambda h, a=carry[:, tsr * 24 + mo * 3: tsr * 24 + (mo + 1) * 3],
                    b=ext_s[:, OC:OC + 3]: h.tensor_copy(a, b))
                if dbg and _dump_cv[0]:
                    dump("cvq0", cv_s, oc, NOC, [t_cv])
                    _dump_cv[0] = False
                if tsr < 2:
                    toks.append(OPt("se", lambda h, a=out_silu[:, mo * OC:(mo + 1) * OC], b=cv_s:
                                    h.activation(a, b, AF.Silu), deps=[t_cv], tag=True))
                else:
                    t_tp = None
                    for i in range(NT):
                        t_tp = OPt("te", lambda h, o=PT0[:, (mo % 4) * 256 + i * 128:(mo % 4) * 256 + (i + 1) * 128],
                                   a=cv_s[:, i * 128:(i + 1) * 128], st=True:
                                   h.matmul(o, a, cb["cI128"], start=st, stop=True, is_transpose=True),
                                   deps=[t_cv], tag=(i == NT - 1))
                    toks.append(t_tp)
                    if mo % 4 == 3:
                        g4 = mo // 4
                        src_ap = PT0.rearrange("p (m i f) -> p m i f", m=4, i=NT)
                        dst_ap = vtm.rearrange("p (i j f) -> p i j f", i=NT, j=8)[:, :, 4 * g4:4 * g4 + 4, :]
                        dst_ap = dst_ap.rearrange("p i m f -> p m i f")
                        toks.append(OPt("se", lambda h, a=dst_ap, b=src_ap:
                                        h.activation(a, b, AF.Silu), deps=[t_tp], tag=True))
            return toks, t_mm

        t_qsil, _ = emit_proj_conv(0, wq_s, qsil, tok_wq)
        t_ksil, _ = emit_proj_conv(1, wk_s, ksil, tok_wk)

        def emit_l2(sil_tile, ps_bank, pbase, dep_list):
            t_bm = None
            for j in range(8):
                t_q2 = OPt("se", lambda h, a=sq_s, b=sil_tile[:, j * OC:(j + 1) * OC]:
                           h.activation(a, b, AF.Square), deps=dep_list + [t_bm], tag=True)
                t_bm = OPt("te", lambda h, o=ps_bank[pbase:pbase + 16, 0:OC],
                           l=cb["cBLK"][:, j * 16:(j + 1) * 16], r=sq_s, st=(j == 0), sp=(j == 7):
                           h.matmul(o, l, r, start=st, stop=sp),
                           deps=[t_q2, tok_F_prev], tag=True)
            return t_bm
        t_qss = emit_l2(qsil, Pb[2], 0, t_qsil)
        t_kss = emit_l2(ksil, Pb[3], 32, t_ksil)
        t_eps_q = OPt("ve", lambda h: h.tensor_scalar(rowsN[0:16, 0:OC], Pb[2][0:16, 0:OC],
                                                      1e-6, None, OP.add), deps=[t_qss], tag=True)
        tok_rq = emit_rsqrt(rowsN[0:16, 0:OC], rowsR[0:16, 0:OC],
                            rowsN[0:16, OC:2 * OC], rowsN[0:16, 2 * OC:3 * OC],
                            deps=[t_eps_q], post_scale=0.125)
        t_eps_k = OPt("ve", lambda h: h.tensor_scalar(rowsN[32:48, 0:OC], Pb[3][32:48, 0:OC],
                                                      1e-6, None, OP.add), deps=[t_kss], tag=True)
        tok_rk = emit_rsqrt(rowsN[32:48, 0:OC], rowsR[32:48, 0:OC],
                            rowsN[32:48, OC:2 * OC], rowsN[32:48, 2 * OC:3 * OC], deps=[t_eps_k])
        tok_rkb = OPt("ve", lambda h: h.tensor_copy(rbA[32:48, 0:OC], rowsR[32:48, 0:OC]),
                      deps=[tok_rk], tag=True)
        t_ap = None
        for j in range(8):
            t_bc = OPt("te", lambda h, o=Pb[4][:, 0:OC], l=cb["cINDFC"][32:48, j * 128:(j + 1) * 128],
                       r=rbA[32:48, 0:OC]: h.matmul(o, l, r, start=True, stop=True),
                       deps=[tok_rkb, t_ap], tag=True)
            t_ap = OPt("ve", lambda h, a=kl2n[:, j * OC:(j + 1) * OC], b=ksil[:, j * OC:(j + 1) * OC],
                       c=Pb[4][:, 0:OC]: h.tensor_tensor(a, b, c, OP.mult), deps=[t_bc], tag=True)
        tok_kl2n = t_ap

        t_ba = None
        for i in range(NT):
            for ko in range(8):
                t_ba = OPt("te", lambda h, o=Pb[5][:, i * 32:(i + 1) * 32],
                           l=h1T[:, ko * OC + i * 128: ko * OC + (i + 1) * 128],
                           r=wba_s[:, ko * 32:(ko + 1) * 32], st=(ko == 0), sp=(ko == 7):
                           h.matmul(o, l, r, start=st, stop=sp),
                           deps=[tok_h1T, tok_wba], tag=(ko == 7))
        pv = Pb[5][:, 0:NT * 32].rearrange("p (i w) -> p i w", w=32)
        tok_bp = OPt("ve", lambda h, a=bg_sb[:, 0:32].rearrange("p (i w) -> p i w", w=16),
                     b=pv[:, :, 0:16]: h.tensor_copy(a, b), deps=[t_ba], tag=True)
        tok_ap_ = OPt("ve", lambda h, a=bg_sb[:, 32:64].rearrange("p (i w) -> p i w", w=16),
                      b=pv[:, :, 16:32]: h.tensor_copy(a, b), deps=[t_ba], tag=True)

        t_z = OPt("ve", lambda h: h.tensor_tensor(g_sb, bg_sb[:, OC // 8:2 * OC // 8],
                                                  cf["cdtb"][:, 0:OC // 8], OP.add),
                  deps=[tok_ap_], tag=True)
        t1 = OPt("se", lambda h: h.activation(gsc_sb, g_sb, AF.Abs), deps=[t_z], tag=True)
        t2 = OPt("se", lambda h: h.activation(gsc2_sb, gsc_sb, AF.Exp, scale=-1.0), tag=True)
        t3 = OPt("ve", lambda h: h.tensor_scalar(gsc2_sb, gsc2_sb, 1.0, None, OP.add),
                 deps=[t2], tag=True)
        t4 = OPt("se", lambda h: h.activation(gsc_sb, gsc2_sb, AF.Ln), deps=[t3], tag=True)
        t5 = OPt("ve", lambda h: h.tensor_scalar(gsc2_sb, g_sb, 0.0, None, OP.max),
                 deps=[t4], tag=True)
        tok_sp = OPt("ve", lambda h: h.tensor_tensor(gsc_sb, gsc_sb, gsc2_sb, OP.add),
                     deps=[t4, t5], tag=True)
        tok_g = OPt("ve", lambda h: h.scalar_tensor_tensor(g_sb, gsc_sb, -1.0, expA[:, 0:32],
                                                           OP.mult, OP.mult),
                    deps=[tok_sp, tok_expA], tag=True)
        u1 = OPt("se", lambda h: h.activation(spb_sb, bg_sb[:, 0:32], AF.Abs),
                 deps=[tok_bp, tok_g], tag=True)
        u2 = OPt("se", lambda h: h.activation(gsc2_sb, spb_sb, AF.Exp, scale=-1.0), tag=True)
        u3 = OPt("ve", lambda h: h.tensor_scalar(gsc2_sb, gsc2_sb, 1.0, None, OP.add),
                 deps=[u2], tag=True)
        u4 = OPt("se", lambda h: h.activation(spb_sb, gsc2_sb, AF.Ln), deps=[u3], tag=True)
        u5 = OPt("ve", lambda h: h.tensor_scalar(gsc2_sb, bg_sb[:, 0:32], 0.0, -1.0, OP.min, OP.mult),
                 deps=[u4], tag=True)
        tok_spb = OPt("ve", lambda h: h.tensor_tensor(spb_sb, spb_sb, gsc2_sb, OP.add),
                      deps=[u4, u5], tag=True)
        tok_gbf = OPt("ve", lambda h: h.tensor_copy(g_bf, g_sb), deps=[tok_g], tag=True)
        tok_ngbf = OPt("ve", lambda h: h.tensor_scalar(ngbf, g_sb, -1.0, None, OP.mult),
                       deps=[tok_g], tag=True)

        tok_wv = DMA("ge", lambda h, b=wview("wv"): h.dma_start(w3(wvg_s, 1024), b), deps=[tok_wv_free])
        t_vsil, t_vmm = emit_proj_conv(2, wvg_s, None, tok_wv, extra_mm_deps=[tok_kl2n])
        tok_vtm = t_vsil[-1]
        tok_wg = DMA("ge", lambda h, b=wview("wg"): h.dma_start(w3(wvg_s, 1024), b), deps=[t_vmm])

        if dbg:
            dump("h1T", h1T, oc, NOC, [tok_h1T])
            dump("qsil", qsil, oc, NOC, t_qsil)
            dump("kl2n", kl2n, oc, NOC, [tok_kl2n])
            dump("vtm", vtm, oc, NOC, [tok_vtm])
            dump("g", g_sb, oc, NOC, [tok_g])
            dump("spb", spb_sb, oc, NOC, [tok_spb])
            dump("bg", bg_sb, oc, NOC, [tok_bp, tok_ap_])
            dump("rsqq", rowsR[0:16, :], oc, NOC, [tok_rq, tok_rk])
            dump("rsqk", rowsR[32:48, :], oc, NOC, [tok_rq, tok_rk])
        LN8 = math.log(0.125)
        t_chunk_prev = None
        t_bprev = None
        for cc in range(CPO):
            tb = cc * 128
            gsl = slice(cc * 16, (cc + 1) * 16)
            t_cs = OPt("te", lambda h, r=g_bf[:, gsl]: h.matmul(Pb[5][:, 0:16], cb["cTU"], r,
                                                               start=True, stop=True),
                       deps=[tok_gbf, tok_bp, tok_ap_, t_bprev], tag=True)
            t_btm = OPt("ve", lambda h: h.tensor_copy(b_tm[:, 0:16], Pb[5][:, 0:16]),
                        deps=[t_cs], tag=True)
            t_bT = OPt("te", lambda h: h.matmul(PT1[0:16, 0:128], b_tm[:, 0:16], cI128f,
                                                start=True, stop=True, is_transpose=True),
                       deps=[t_btm, t_chunk_prev], tag=True)
            t_bpT = OPt("te", lambda h, a=bg_sb[:, cc * 16:(cc + 1) * 16]:
                        h.matmul(PT1[0:16, 128:256], a, cI128f, start=True, stop=True,
                                 is_transpose=True), deps=[tok_bp], tag=True)
            t_spT = OPt("te", lambda h, a=spb_sb[:, cc * 16:(cc + 1) * 16]:
                        h.matmul(PT1[0:16, 256:384], a, cI128f, start=True, stop=True,
                                 is_transpose=True), deps=[tok_spb], tag=True)
            bT = rowsA[0:16, 0:128]
            t_c1 = OPt("ve", lambda h: h.tensor_copy(bT, PT1[0:16, 0:128]),
                       deps=[t_bT, t_bpT, t_spT], tag=True)
            t_c2 = OPt("ve", lambda h: h.tensor_copy(rowsA[0:16, 256:384], PT1[0:16, 128:256]),
                       deps=[t_spT], tag=True)
            t_rows0 = OPt("ve", lambda h: h.tensor_copy(rowsA[0:16, 384:512], PT1[0:16, 256:384]),
                          deps=[t_spT], tag=True)
            eb = rowsA[0:16, 128:256]
            t_eb = OPt("se", lambda h: h.activation(eb, bT, AF.Exp), deps=[t_c1], tag=True)
            t_ebq = OPt("se", lambda h, a=rbA[0:16, 128:256]:
                        h.activation(a, bT, AF.Exp),
                        deps=[tok_ln8, t_c1], tag=True)
            t_d1 = OPt("ve", lambda h, a=rowsN[0:16, 0:128]:
                       h.tensor_scalar(a, bT, rowsA[0:16, 127:128], None, OP.subtract),
                       deps=[t_c1], tag=True)
            t_ecmb = OPt("se", lambda h, a=rbA[0:16, 512:640], b=rowsN[0:16, 0:128]:
                         h.activation(a, b, AF.Exp, scale=-1.0), deps=[t_d1], tag=True)
            t_dc = OPt("se", lambda h, a=rowsN[0:16, 256:257], b=rowsA[0:16, 127:128]:
                       h.activation(a, b, AF.Exp), deps=[t_c1], tag=True)
            t_th = OPt("se", lambda h, a=rowsN[0:16, 128:256], b=rowsA[0:16, 256:384]:
                       h.activation(a, b, AF.Tanh, scale=0.5), deps=[t_c2], tag=True)
            t_br = OPt("ve", lambda h, a=rowsN[0:16, 128:256]:
                       h.tensor_scalar(a, a, 0.5, 0.5, OP.mult, OP.add), deps=[t_th], tag=True)
            OPt("ve", lambda h, a=rbA[0:16, 384:512], b=rowsN[0:16, 128:256]:
                h.tensor_scalar(a, b, -1.0, None, OP.mult), deps=[t_br])
            OPt("ve", lambda h, a=rbA[0:16, 256:384], b=rowsN[0:16, 128:256]:
                h.scalar_tensor_tensor(a, b, -1.0, eb, OP.mult, OP.mult), deps=[t_eb])
            OPt("ve", lambda h, a=rbA[0:16, 0:128], b=rowsA[0:16, 384:512]:
                h.tensor_scalar(a, b, -1.0, None, OP.mult), deps=[t_rows0])
            t_rows = OPt("ve", lambda h, a=rbB[0:16, 0:512], b=rowsN[0:16, 256:257]:
                         h.tensor_scalar(a, cb["cIND64P"][0:16, :], b, None, OP.mult),
                         deps=[t_dc, t_ecmb, t_ebq], tag=True)
            t_qgm = None
            for j in range(8):
                t_qgm = OPt("te", lambda h, o=Pb[j // 4][:, (j % 4) * 128:(j % 4 + 1) * 128],
                            l=cb["cINDFC"][0:16, j * 128:(j + 1) * 128], r=rbA[0:16, 128:256]:
                            h.matmul(o, l, r, start=True, stop=True),
                            deps=[t_rows, t_chunk_prev], tag=(j in (3, 7)))
            qv = qsil.rearrange("p (j t) -> p j t", j=8)
            OPt("ve", lambda h, a=Qgc[:, 0:512].rearrange("p (j t) -> p j t", j=4),
                b=qv[:, 0:4, tb:tb + 128], c=Pb[0].rearrange("p (j t) -> p j t", j=4):
                h.tensor_tensor(a, b, c, OP.mult), deps=[t_qgm] + t_qsil)
            t_qg = OPt("ve", lambda h, a=Qgc[:, 512:1024].rearrange("p (j t) -> p j t", j=4),
                       b=qv[:, 4:8, tb:tb + 128], c=Pb[1].rearrange("p (j t) -> p j t", j=4):
                       h.tensor_tensor(a, b, c, OP.mult), tag=True)
            tuv = cb["cTU"].rearrange("p (o f) -> p o f", o=1).broadcast_to([128, 16, 128])
            t_gm = OPt("ve", lambda h, a=gm_s.rearrange("p (w s) -> p w s", w=16),
                       b=g_bf[:, gsl].broadcast_to([128, 16, 128]), c=tuv:
                       h.tensor_tensor(a, b, c, OP.mult), deps=[tok_gbf, t_qg], tag=True)
            t_dl = None
            for b4 in range(4):
                hsl = slice(cc * 16 + 4 * b4, cc * 16 + 4 * b4 + 4)
                gv4 = g_bf[:, hsl].broadcast_to([128, 4, 128])
                tub4 = cb["cTUBIG"].rearrange("p (o f) -> p o f", o=1).broadcast_to([128, 4, 128])
                OPt("te", lambda h, o=Pb[b4].rearrange("p (a s) -> p a s", a=4), r=gv4:
                    h.matmul(o, cb["cTU"], r, start=True, stop=False),
                    deps=[t_gm, t_qg, tok_rq, tok_rk])
                OPt("te", lambda h, o=Pb[b4], r=gm_s[:, b4 * 512:(b4 + 1) * 512]:
                    h.matmul(o, cb["cNEG1"], r, start=False, stop=False))
                OPt("te", lambda h, o=Pb[b4], l=rbA[0:16, 0:128],
                    r=cb["cIND128"][0:16, b4 * 512:(b4 + 1) * 512]:
                    h.matmul(o, l, r, start=False, stop=False))
                t_dl = OPt("te", lambda h, o=Pb[b4].rearrange("p (a s) -> p a s", a=4), r=tub4:
                           h.matmul(o, cb["cI128"], r, start=False, stop=True), tag=True)
            t_el = None
            for b4 in range(4):
                t_el = OPt("se", lambda h, a=E_s[:, b4 * 512:(b4 + 1) * 512], b=Pb[b4]:
                           h.activation(a, b, AF.Exp), deps=[t_dl], tag=(b4 == 3))
            def hsl_ap(tile, hh, width=128):
                return tile[(hh % 2) * 64:(hh % 2) * 64 + 64,
                            (hh // 2) * OC + tb:(hh // 2) * OC + tb + width]
            t_kk = None
            for hh in range(16):
                t_kk = OPt("te", lambda h, o=Pb[hh // 4][:, (hh % 4) * 128:(hh % 4 + 1) * 128],
                           l=hsl_ap(kl2n, hh), r=hsl_ap(kl2n, hh):
                           h.matmul(o, l, r, start=True, stop=True),
                           deps=[tok_kl2n, t_el], tag=(hh % 4 == 3))
            t_kkc = None
            for b4 in range(4):
                t_kkc = OPt("se", lambda h, a=KKQ[:, b4 * 512:(b4 + 1) * 512], b=Pb[b4]:
                            h.activation(a, b, AF.Copy), deps=[t_kk], tag=(b4 == 3))
            OPt("ve", lambda h: h.tensor_tensor(L_s[:, 0:1024], KKQ[:, 0:1024], E_s[:, 0:1024], OP.mult),
                deps=[t_kkc])
            t_L = OPt("ve", lambda h: h.tensor_tensor(L_s[:, 1024:2048], KKQ[:, 1024:2048],
                                                      E_s[:, 1024:2048], OP.mult), tag=True)
            t_dt = None
            for b4 in range(4):
                hsl = slice(cc * 16 + 4 * b4, cc * 16 + 4 * b4 + 4)
                ngv4 = ngbf[:, hsl].broadcast_to([128, 4, 128])
                tlb4 = cb["cTLBIG"].rearrange("p (o f) -> p o f", o=1).broadcast_to([128, 4, 128])
                OPt("te", lambda h, o=Pb[b4], r=gm_s[:, b4 * 512:(b4 + 1) * 512]:
                    h.matmul(o, cb["cONES"], r, start=True, stop=False),
                    deps=[t_L, tok_ngbf, t_kkc])
                OPt("te", lambda h, o=Pb[b4].rearrange("p (a s) -> p a s", a=4), r=ngv4:
                    h.matmul(o, cb["cTU"], r, start=False, stop=False))
                t_dt = OPt("te", lambda h, o=Pb[b4].rearrange("p (a s) -> p a s", a=4), r=tlb4:
                           h.matmul(o, cb["cI128"], r, start=False, stop=True), tag=True)
            t_et = None
            for b4 in range(4):
                t_et = OPt("se", lambda h, a=E_s[:, b4 * 512:(b4 + 1) * 512], b=Pb[b4]:
                           h.activation(a, b, AF.Exp), deps=[t_dt], tag=(b4 == 3))
            t_kq = None
            for hh in range(16):
                t_kq = OPt("te", lambda h, o=Pb[hh // 4][:, (hh % 4) * 128:(hh % 4 + 1) * 128],
                           l=hsl_ap(kl2n, hh), r=hsl_ap(qsil, hh):
                           h.matmul(o, l, r, start=True, stop=True),
                           deps=[t_et], tag=(hh % 4 == 3))
            t_kqc = None
            for b4 in range(4):
                t_kqc = OPt("se", lambda h, a=KKQ[:, b4 * 512:(b4 + 1) * 512], b=Pb[b4]:
                            h.activation(a, b, AF.Copy), deps=[t_kq], tag=(b4 == 3))
            OPt("ve", lambda h: h.tensor_tensor(AT_s[:, 0:1024], KKQ[:, 0:1024], E_s[:, 0:1024], OP.mult),
                deps=[t_kqc])
            t_AT = OPt("ve", lambda h: h.tensor_tensor(AT_s[:, 1024:2048], KKQ[:, 1024:2048],
                                                       E_s[:, 1024:2048], OP.mult), tag=True)
            t_ktp = None
            for j in range(8):
                t_ktp = OPt("te", lambda h, o=PT0[:, j * 128:(j + 1) * 128],
                            a=kl2n[:, j * OC + tb:j * OC + tb + 128], st=True:
                            h.matmul(o, a, cb["cI128"], start=st, stop=True, is_transpose=True),
                            deps=[tok_kl2n, t_kqc], tag=(j == 7))
            t_ktm = OPt("ve", lambda h: h.tensor_copy(ktm_s, PT0), deps=[t_ktp], tag=True)
            t_r2m = None
            for b4 in range(4):
                OPt("te", lambda h, o=Pb[b4], l=rbA[0:16, 256:384],
                    r=cb["cINDW"][0:16, b4 * 512:(b4 + 1) * 512]:
                    h.matmul(o, l, r, start=True, stop=False), deps=[t_rows, t_AT, t_kqc])
                t_r2m = OPt("te", lambda h, o=Pb[b4], l=rbA[0:16, 384:512],
                            r=cb["cINDU"][0:16, b4 * 512:(b4 + 1) * 512]:
                            h.matmul(o, l, r, start=False, stop=True), tag=(b4 == 3))
            t_r2 = None
            for b4 in range(4):
                OPt("ve", lambda h,
                    a=Ua[:, b4 * 512:(b4 + 1) * 512].rearrange("p (q w) -> p q w", q=4)[:, :, 0:64],
                    b=ktm_s[:, b4 * 256:(b4 + 1) * 256].rearrange("p (q w) -> p q w", q=4),
                    c=Pb[b4].rearrange("p (q w) -> p q w", q=4)[:, :, 0:64]:
                    h.tensor_tensor(a, b, c, OP.mult), deps=[t_r2m, t_ktm])
                t_r2 = OPt("ve", lambda h,
                           a=Ua[:, b4 * 512:(b4 + 1) * 512].rearrange("p (q w) -> p q w", q=4)[:, :, 64:128],
                           b=vtm[:, cc * 1024 + b4 * 256: cc * 1024 + (b4 + 1) * 256].rearrange(
                               "p (q w) -> p q w", q=4),
                           c=Pb[b4].rearrange("p (q w) -> p q w", q=4)[:, :, 64:128]:
                           h.tensor_tensor(a, b, c, OP.mult), deps=[tok_vtm], tag=(b4 == 3))
            if dbg:
                dump("rhs2", Ua, oc * CPO + cc, NOC * CPO, [t_r2])

            t_ztp = None
            t_zev = None
            for rnd in range(2):
                for k in range(8):
                    hh = 8 * rnd + k
                    t_ztp = OPt("te", lambda h, o=PT0[:, k * 128:(k + 1) * 128],
                                a=L_s[:, hh * 128:(hh + 1) * 128], st=True:
                                h.matmul(o, a, cb["cI128"], start=st, stop=True,
                                         is_transpose=True),
                                deps=[t_L, t_zev, t_r2m, t_ktm], tag=(k == 7))
                OPt("ve", lambda h, a=Z_s[:, rnd * 1024:(rnd + 1) * 1024], b=PT0:
                    h.tensor_copy(a, b), deps=[t_ztp])
                t_zev = OPt("ve", lambda h, a=nZ0[:, rnd * 1024:(rnd + 1) * 1024], b=PT0:
                            h.tensor_scalar(a, b, -1.0, None, OP.mult), deps=[t_ztp], tag=True)
            NLVL = 6
            def solve_block(blk, dep0):
                ps = slice(64 * blk, 64 * blk + 64)
                ident = cb["cI128"][ps, ps]
                cur, nxt = Ua, Ub
                t_app = None
                t_ch = None
                for lvl in range(NLVL):
                    t_am = None
                    for hh in range(16):
                        if lvl == 0:
                            zsl = nZ0[ps, hh * 128 + 64 * blk:hh * 128 + 64 * blk + 64]
                        else:
                            zt = Z_s if lvl % 2 == 0 else nZ0
                            zsl = zt[ps, hh * 128:hh * 128 + 64]
                        OPt("te", lambda h, o=Pb[hh // 4][ps, (hh % 4) * 128:(hh % 4 + 1) * 128],
                            l=ident, r=cur[ps, hh * 128:(hh + 1) * 128]:
                            h.matmul(o, l, r, start=True, stop=False),
                            deps=[dep0, t_app, t_ch, t_zev])
                        t_am = OPt("te", lambda h, o=Pb[hh // 4][ps, (hh % 4) * 128:(hh % 4 + 1) * 128],
                                   l=zsl, r=cur[ps, hh * 128:(hh + 1) * 128]:
                                   h.matmul(o, l, r, start=False, stop=True), tag=(hh % 4 == 3))
                    neg = (lvl == NLVL - 1)
                    for q in range(4):
                        t_app = OPt("ve", lambda h, a=nxt[ps, q * 512:(q + 1) * 512],
                                    b=Pb[q][ps, :], ng=neg:
                                    (h.tensor_scalar(a, b, -1.0, None, OP.mult) if ng
                                     else h.tensor_copy(a, b)), deps=[t_am], tag=(q == 3))
                    cur, nxt = nxt, cur
                    if lvl < NLVL - 1:
                        noff = 64 * ((lvl + 1) % 2)
                        yoff = 64 * (lvl % 2)
                        zt_cur = Z_s if lvl % 2 == 0 else nZ0
                        zt_nxt = Z_s if (lvl + 1) % 2 == 0 else nZ0
                        def ysl(hh, lvl=lvl, yoff=yoff):
                            if lvl == 0:
                                return L_s[ps, hh * 128 + 64 * blk:hh * 128 + 64 * blk + 64]
                            return Y_s[ps, hh * 128 + yoff:hh * 128 + yoff + 64]
                        def zsl0(hh, zt_cur=zt_cur, lvl=lvl):
                            if lvl == 0:
                                return zt_cur[ps, hh * 128 + 64 * blk:hh * 128 + 64 * blk + 64]
                            return zt_cur[ps, hh * 128:hh * 128 + 64]
                        t_ym = None
                        for hh in range(16):
                            pbk = Pb[4] if hh < 8 else Pb[5]
                            t_ym = OPt("te", lambda h, o=pbk[ps, (hh % 8) * 64:(hh % 8 + 1) * 64],
                                       l=zsl0(hh), r=ysl(hh):
                                       h.matmul(o, l, r, start=True, stop=True),
                                       deps=[t_ch, t_app], tag=(hh % 8 == 7))
                        t_ye = None
                        for q in range(2):
                            t_ye = OPt("ve", lambda h,
                                       a=Y_s[ps, q * 1024:(q + 1) * 1024].rearrange(
                                           "p (e w) -> p e w", w=128)[:, :, noff:noff + 64],
                                       b=Pb[4 + q][ps, :].rearrange("p (e w) -> p e w", w=64):
                                       h.tensor_copy(a, b), deps=[t_ym], tag=(q == 1))
                        t_zm = None
                        for hh in range(16):
                            pbk = Pb[4] if hh < 8 else Pb[5]
                            t_zm = OPt("te", lambda h, o=pbk[ps, (hh % 8) * 64:(hh % 8 + 1) * 64],
                                       l=ysl(hh), r=zsl0(hh):
                                       h.matmul(o, l, r, start=True, stop=True),
                                       deps=[t_ye], tag=(hh % 8 == 7))
                        t_ch = None
                        for q in range(2):
                            t_ch = OPt("ve", lambda h,
                                       a=zt_nxt[ps, q * 1024:(q + 1) * 1024].rearrange(
                                           "p (e w) -> p e w", w=128)[:, :, 0:64],
                                       b=Pb[4 + q][ps, :].rearrange("p (e w) -> p e w", w=64):
                                       h.tensor_copy(a, b), deps=[t_zm], tag=(q == 1))
                return t_app
            t_x1 = solve_block(0, t_r2)
            if dbg:
                dump("preV0", Ua, oc * CPO + cc, NOC * CPO, [t_x1])
                dump("nZ0d", nZ0, oc * CPO + cc, NOC * CPO, [t_x1])
            t_v0m = None
            for hh in range(16):
                OPt("te", lambda h, o=Pb[hh // 4][64:128, (hh % 4) * 128:(hh % 4 + 1) * 128],
                    l=cb["cI128"][64:128, 64:128], r=Ua[64:128, hh * 128:(hh + 1) * 128]:
                    h.matmul(o, l, r, start=True, stop=False),
                    deps=[t_x1, t_r2])
                t_v0m = OPt("te", lambda h, o=Pb[hh // 4][64:128, (hh % 4) * 128:(hh % 4 + 1) * 128],
                            l=Z_s[0:64, hh * 128 + 64:hh * 128 + 128],
                            r=Ua[0:64, hh * 128:(hh + 1) * 128]:
                            h.matmul(o, l, r, start=False, stop=True), tag=(hh % 4 == 3))
            t_v0 = None
            for q in range(4):
                t_v0 = OPt("ve", lambda h, a=Ua[64:128, q * 512:(q + 1) * 512], b=Pb[q][64:128, :]:
                           h.tensor_copy(a, b), deps=[t_v0m], tag=(q == 3))
            if dbg:
                dump("V0b2", Ua[64:128, :], oc * CPO + cc, NOC * CPO, [t_v0])
                dump("X1d", Ua[0:64, :], oc * CPO + cc, NOC * CPO, [t_x1])
            t_x2 = solve_block(1, t_v0)
            tok_X = t_x2

            t_kdm = None
            for q in range(2):
                t_kdm = OPt("te", lambda h, o=Pb[4 + q], l=rbA[0:16, 512:640],
                            r=cb["cIND64"][0:16, q * 512:(q + 1) * 512]:
                            h.matmul(o, l, r, start=True, stop=True),
                            deps=[tok_X, t_rows], tag=(q == 1))
            t_kd = None
            for q in range(2):
                t_kd = OPt("ve", lambda h, a=Kdec_s[:, q * 512:(q + 1) * 512],
                           b=ktm_s[:, q * 512:(q + 1) * 512], c=Pb[4 + q]:
                           h.tensor_tensor(a, b, c, OP.mult), deps=[t_kdm, t_ktm], tag=(q == 1))
            t_jm = None
            for hh in range(16):
                t_jm = OPt("te", lambda h,
                           o=Pb[hh // 8][(hh % 2) * 64:(hh % 2) * 64 + 64,
                                         ((hh % 8) // 2) * 128:((hh % 8) // 2 + 1) * 128],
                           l=Ua[:, hh * 128:hh * 128 + 64],
                           r=AT_s[:, hh * 128:(hh + 1) * 128]:
                           h.matmul(o, l, r, start=True, stop=True),
                           deps=[tok_X, t_AT], tag=(hh % 8 == 7))
            OPt("ve", lambda h: h.tensor_sub(JT_s[:, 0:512], Qgc[:, 0:512], Pb[0]), deps=[t_jm, t_qg])
            t_JT = OPt("ve", lambda h: h.tensor_sub(JT_s[:, 512:1024], Qgc[:, 512:1024], Pb[1]),
                       deps=[t_jm], tag=True)
            t_gm2 = None
            for hh in range(16):
                t_gm2 = OPt("te", lambda h,
                            o=Pb[2][(hh % 2) * 64:(hh % 2) * 64 + 64,
                                    (hh // 2) * 64:(hh // 2 + 1) * 64],
                            l=Ua[:, hh * 128:hh * 128 + 64],
                            r=Kdec_s[:, hh * 64:(hh + 1) * 64]:
                            h.matmul(o, l, r, start=True, stop=True),
                            deps=[t_kd, tok_X], tag=(hh == 15))
            t_ngt = OPt("ve", lambda h: h.tensor_scalar(GT_s, Pb[2], -1.0, None, OP.mult),
                        deps=[t_gm2], tag=True)
            t_sbf = OPt("ve", lambda h: h.tensor_copy(Sbf, S_sb), deps=[t_chunk_prev], tag=True)
            t_hm = None
            for hh in range(16):
                OPt("te", lambda h,
                    o=Pb[3][(hh % 2) * 64:(hh % 2) * 64 + 64, (hh // 2) * 64:(hh // 2 + 1) * 64],
                    l=Kdec_s[:, hh * 64:(hh + 1) * 64],
                    r=Ua[:, hh * 128 + 64:hh * 128 + 128]:
                    h.matmul(o, l, r, start=True, stop=False), deps=[t_kd, tok_X, t_sbf])
                t_hm = OPt("te", lambda h,
                           o=Pb[3][(hh % 2) * 64:(hh % 2) * 64 + 64, (hh // 2) * 64:(hh // 2 + 1) * 64],
                           l=GT_s[(hh % 2) * 64:(hh % 2) * 64 + 64, (hh // 2) * 64:(hh // 2 + 1) * 64],
                           r=Sbf[(hh % 2) * 64:(hh % 2) * 64 + 64, (hh // 2) * 64:(hh // 2 + 1) * 64]:
                           h.matmul(o, l, r, start=False, stop=True),
                           deps=[t_ngt], tag=(hh == 15))
            t_ebc = OPt("te", lambda h: h.matmul(Pb[4], cb["cPAIRH"][0:16, :], rbB[0:16, 0:512],
                                                 start=True, stop=True),
                        deps=[t_rows, t_kdm, t_kd], tag=True)
            OPt("ve", lambda h: h.tensor_tensor(Sscr, S_sb, Pb[4], OP.mult), deps=[t_ebc, t_hm])
            t_S = OPt("ve", lambda h: h.tensor_tensor(S_sb, Sscr, Pb[3], OP.add), tag=True)
            t_om = None
            for hh in range(16):
                OPt("te", lambda h,
                    o=Pb[hh // 8][(hh % 2) * 64:(hh % 2) * 64 + 64,
                                  ((hh % 8) // 2) * 128:((hh % 8) // 2 + 1) * 128],
                    l=Sbf[(hh % 2) * 64:(hh % 2) * 64 + 64, (hh // 2) * 64:(hh // 2 + 1) * 64],
                    r=JT_s[(hh % 2) * 64:(hh % 2) * 64 + 64, (hh // 2) * 128:(hh // 2 + 1) * 128]:
                    h.matmul(o, l, r, start=True, stop=False),
                    deps=[t_JT, t_sbf, t_S])
                t_om = OPt("te", lambda h,
                           o=Pb[hh // 8][(hh % 2) * 64:(hh % 2) * 64 + 64,
                                         ((hh % 8) // 2) * 128:((hh % 8) // 2 + 1) * 128],
                           l=Ua[:, hh * 128 + 64:hh * 128 + 128],
                           r=AT_s[:, hh * 128:(hh + 1) * 128]:
                           h.matmul(o, l, r, start=False, stop=True), tag=(hh % 8 == 7))
            ofv = o_fm.rearrange("p (j t) -> p j t", j=8)
            OPt("ve", lambda h, a=ofv[:, 0:4, tb:tb + 128],
                b=Pb[0].rearrange("p (j t) -> p j t", j=4): h.tensor_copy(a, b), deps=[t_om])
            t_chunk_prev = OPt("ve", lambda h, a=ofv[:, 4:8, tb:tb + 128],
                               b=Pb[1].rearrange("p (j t) -> p j t", j=4):
                               h.tensor_copy(a, b), deps=[t_om], tag=True)
            t_bprev = t_chunk_prev
            if dbg:
                ci = oc * CPO + cc
                nci = NOC * CPO
                dump("L", L_s, ci, nci, [t_L])
                dump("AT", AT_s, ci, nci, [t_AT])
                dump("X", Ua, ci, nci, [tok_X])
                dump("Qgc", Qgc, ci, nci, [t_qg])
                dump("JTd", JT_s, ci, nci, [t_JT])
                dump("Kdecd", Kdec_s, ci, nci, [t_kd])
                dump("Sd", S_sb, ci, nci, [t_S])
                dump("rows", rowsA[0:16, :], ci, nci, [t_rows])
                dump("rbAd", rbA[0:16, 0:640], ci, nci, [t_rows])
        tok_scan = t_chunk_prev

        t_bm = None
        for j in range(8):
            t_q2 = OPt("se", lambda h, a=sq_s, b=o_fm[:, j * OC:(j + 1) * OC]:
                       h.activation(a, b, AF.Square), deps=[tok_scan, t_bm], tag=True)
            t_bm = OPt("te", lambda h, o=Pb[2][0:16, 0:OC],
                       l=cb["cBLK"][:, j * 16:(j + 1) * 16], r=sq_s, st=(j == 0), sp=(j == 7):
                       h.matmul(o, l, r, start=st, stop=sp), deps=[t_q2], tag=True)
        OPt("ve", lambda h: h.tensor_scalar(rowsN[0:16, 0:OC], Pb[2][0:16, 0:OC],
                                            1.0 / 64.0, None, OP.mult), deps=[t_bm])
        OPt("ve", lambda h: h.tensor_mul(rowsN[0:16, OC:2 * OC], rowsR[0:16, 0:OC], rowsR[0:16, 0:OC]),
            deps=[tok_rq])
        OPt("ve", lambda h: h.tensor_mul(rowsN[0:16, 0:OC], rowsN[0:16, 0:OC], rowsN[0:16, OC:2 * OC]))
        t_eps_f = OPt("ve", lambda h: h.tensor_scalar(rowsN[0:16, 0:OC], rowsN[0:16, 0:OC],
                                                      1e-5, None, OP.add), tag=True)
        emit_rsqrt(rowsN[0:16, 0:OC], rowsA[0:16, 256:256 + OC],
                   rowsN[0:16, OC:2 * OC], rowsA[0:16, 0:OC], deps=[t_eps_f])
        OPt("ve", lambda h: h.tensor_mul(rowsN[0:16, 0:OC], rowsA[0:16, 256:256 + OC],
                                         rowsR[0:16, 0:OC]))
        t_F = OPt("ve", lambda h: h.tensor_copy(rbA[0:16, 128:128 + OC], rowsN[0:16, 0:OC]), tag=True)
        tok_F_prev = t_F
        t_og = None
        t_gmm = None
        for j in range(8):
            for ko in range(8):
                t_gmm = OPt("te", lambda h, o=Pb[4][:, 0:OC],
                            l=wvg_s[:, ko * 1024 + j * 128: ko * 1024 + (j + 1) * 128],
                            r=h1T[:, ko * OC:(ko + 1) * OC], st=(ko == 0), sp=(ko == 7):
                            h.matmul(o, l, r, start=st, stop=sp),
                            deps=[tok_wg, tok_h1T, t_og, t_bm], tag=(ko == 7))
            t_sg = OPt("se", lambda h, a=sgs[:, 0:OC], b=Pb[4][:, 0:OC]:
                       h.activation(a, b, AF.Silu), deps=[t_gmm], tag=True)
            t_fb = OPt("te", lambda h, o=Pb[5][:, 0:OC],
                       l=cb["cINDFC"][0:16, j * 128:(j + 1) * 128], r=rbA[0:16, 128:128 + OC]:
                       h.matmul(o, l, r, start=True, stop=True), deps=[t_F, t_sg], tag=True)
            OPt("ve", lambda h, a=og[:, j * OC:(j + 1) * OC], b=o_fm[:, j * OC:(j + 1) * OC],
                c=Pb[5][:, 0:OC]: h.tensor_tensor(a, b, c, OP.mult), deps=[t_fb])
            t_og = OPt("ve", lambda h, a=og[:, j * OC:(j + 1) * OC], c=sgs[:, 0:OC]:
                       h.tensor_tensor(a, a, c, OP.mult), deps=[t_sg], tag=True)
        tok_gate_mm = t_gmm
        if dbg:
            dump("ofm", o_fm, oc, NOC, [tok_scan])
            dump("og", og, oc, NOC, [t_og])
            dump("F", rowsN[0:16, :], oc, NOC, [t_F])
        t_resid = None
        for i in range(NT):
            t_wm = None
            for nh in range(2):
                for ko in range(8):
                    t_wm = OPt("te", lambda h, o=Pb[nh][:, 0:512],
                               l=og[:, ko * OC + i * 128: ko * OC + (i + 1) * 128],
                               r=wo_s[:, ko * 1024 + nh * 512: ko * 1024 + (nh + 1) * 512],
                               st=(ko == 0), sp=(ko == 7): h.matmul(o, l, r, start=st, stop=sp),
                               deps=[t_og, tok_wop, t_resid], tag=(ko == 7 and nh == 1))
            for nh in range(2):
                t_resid = OPt("ve", lambda h, a=x_sb[:, i * 1024 + nh * 512: i * 1024 + (nh + 1) * 512],
                              b=Pb[nh][:, 0:512]: h.tensor_tensor(a, a, b, OP.add),
                              deps=[t_wm], tag=(nh == 1))
        tok_resid = t_resid
        tok_store = DMA("sy", lambda h, o=xmid_d[oc * OC:(oc + 1) * OC, :].rearrange(
            "(i p) f -> p i f", p=128): h.dma_start(o, x3), deps=[tok_resid])
        tok_wv_free = tok_gate_mm
    tok_p1_done = tok_store

    bar = [tok_resid, tok_p1_done]
    t_w2l = []
    for dst, nm, view in [
        (wenc_s.rearrange("p (ko n) -> p ko n", n=2048), "wenc",
         dr["wenc"].rearrange("(ko p) n -> p ko n", p=128)),
        (wdec_s.rearrange("p (ko n) -> p ko n", n=1024), "wdec",
         dr["wdec"].rearrange("(ko p) n -> p ko n", p=128)),
        (ws2c_s.rearrange("p (ko n) -> p ko n", n=64), "ws2c",
         dr["ws2c"].rearrange("(ko p) n -> p ko n", p=128)),
        (w1p_s, "w1p", dr["w1p"]),
        (w2p_s, "w2p", dr["w2p"]),
        (wc2s_s[0:64, :], "wc2s", dr["wc2s"]),
        (wr_s[0:64, 0:8], "wr", dr["wr"]),
        (br_s[0:1, 0:8], "br_row", dr["br_row"]),
        (b2r_s[0:1, :], "b2_row", dr["b2_row"]),
    ]:
        t_w2l.append(DMA("ge", lambda h, a=dst, b=view: h.dma_start(a, b), deps=bar))
    tok_w2 = t_w2l[-1]
    t_benc = DMA("sy", lambda h: h.dma_start(benc_h, dr["benc"]), deps=bar)
    t_bc2s = DMA("sy", lambda h: h.dma_start(bc2s_h, dr["bc2s"]), deps=bar)
    t_b1p = DMA("sy", lambda h: h.dma_start(b1p_s, dr["b1p"]), deps=bar)
    OPt("ve", lambda h: h.tensor_scalar(benc_h, benc_h, 0.5, None, OP.mult), deps=[t_benc, bar[0]])
    OPt("ve", lambda h: h.tensor_scalar(bc2s_h, bc2s_h, 0.5, None, OP.mult), deps=[t_bc2s])
    OPt("ve", lambda h: h.tensor_scalar(ws2c_s, ws2c_s, 0.5, None, OP.mult), deps=[t_w2l[2]])
    tok_wpre = OPt("ve", lambda h: h.tensor_scalar(wdec_s, wdec_s, 0.5, None, OP.mult),
                   deps=[t_w2l[1]], tag=True)
    t_cs = None
    for ko in range(16):
        t_cs = OPt("te", lambda h, l=ws2c_s[:, ko * 64:(ko + 1) * 64], st=(ko == 0), sp=(ko == 15):
                   h.matmul(PT1[0:64, 0:1], l, cb["cONESCOL"], start=st, stop=sp),
                   deps=[tok_wpre, tok_cb], tag=(ko == 15))
    t_cse = OPt("ve", lambda h: h.tensor_copy(tk_f[0:64, 0:1], PT1[0:64, 0:1]), deps=[t_cs], tag=True)
    t_cst = OPt("te", lambda h: h.transpose(PT1[0:1, 64:128], tk_f[0:64, 0:1], cI128f[0:64, 0:64]),
                deps=[t_cse], tag=True)
    t_bs2c = DMA("sy", lambda h: h.dma_start(rowsA[0:1, 0:64], dr["bs2c_row"]), deps=bar)
    tok_bs2cr = OPt("ve", lambda h: h.scalar_tensor_tensor(bs2cr[0:1, :], PT1[0:1, 64:128], 1.0,
                                                           rowsA[0:1, 0:64], OP.mult, OP.add),
                    deps=[t_cst, t_bs2c], tag=True)
    t_cd = None
    for no in range(8):
        for ko in range(16):
            t_cd = OPt("te", lambda h, o=PT1[0:128, no + 2:no + 3],
                       l=wdec_s[:, ko * 1024 + no * 128: ko * 1024 + (no + 1) * 128],
                       st=(ko == 0), sp=(ko == 15): h.matmul(o, l, cb["cONESCOL"], start=st, stop=sp),
                       deps=[tok_wpre, t_cst, tok_bs2cr], tag=(ko == 15))
    t_cde = OPt("ve", lambda h: h.tensor_copy(tk_f[:, 8:16], PT1[:, 2:10]), deps=[t_cd], tag=True)
    t_bd1 = DMA("sy", lambda h: h.dma_start(rowsA[0:1, 0:512], dr["bdec_row"][0:1, 0:512]),
                deps=[tok_bs2cr])
    t_bd2 = DMA("sy", lambda h: h.dma_start(rowsN[0:1, 0:512], dr["bdec_row"][0:1, 512:1024]),
                deps=[tok_bs2cr])
    tok_bdec = None
    for no in range(8):
        t_dtp = OPt("te", lambda h, a=tk_f[:, 8 + no:9 + no]:
                    h.matmul(PT1[0:1, 128:256], a, cI128f, start=True, stop=True,
                             is_transpose=True),
                    deps=[t_cde, tok_bdec], tag=True)
        srcrow = rowsA[0:1, (no % 4) * 128:(no % 4 + 1) * 128] if no < 4 else \
                 rowsN[0:1, ((no - 4) % 4) * 128:((no - 4) % 4 + 1) * 128]
        tok_bdec = OPt("ve", lambda h, a=bdecr[0:1, no * 128:(no + 1) * 128],
                       b=PT1[0:1, 128:256], c=srcrow:
                       h.scalar_tensor_tensor(a, b, 1.0, c, OP.mult, OP.add),
                       deps=[t_dtp, t_bd1, t_bd2], tag=True)

    tok_out = None
    tok_res2 = None
    for oc in range(NOC):
        deps0 = [d for d in (tok_res2, tok_out) if d is not None]
        t_x2 = DMA("sy", lambda h, b=xmid_d[oc * OC:(oc + 1) * OC, :].rearrange(
            "(i p) f -> p i f", p=128): h.dma_start(x3, b),
            deps=deps0 + ([tok_p1_done, tok_bdec] if oc == 0 else []))
        t_sq = []
        t_sm = []
        for i in range(NT):
            xi = x_sb[:, i * 1024:(i + 1) * 1024]
            t_sq.append(OPt("se", lambda h, a=smt_s[:, 0:1024], b=xi, acc=stat[:, 4 + i:5 + i]:
                            h.activation(a, b, AF.Square, accum_out=acc), deps=[t_x2], tag=True))
            t_sm.append(OPt("ve", lambda h, a=stat[:, i:i + 1], b=xi:
                            h.reduce_sum(a, b, AX.X), deps=[t_x2], tag=True))
        OPt("ve", lambda h: h.tensor_scalar(stat[:, 8:8 + NT], stat[:, 0:NT], INV_D, None, OP.mult),
            deps=t_sq + t_sm)
        OPt("ve", lambda h: h.tensor_mul(stat[:, 12:12 + NT], stat[:, 8:8 + NT], stat[:, 8:8 + NT]))
        OPt("ve", lambda h: h.tensor_scalar(stat[:, 16:16 + NT], stat[:, 4:4 + NT], INV_D, None, OP.mult))
        OPt("ve", lambda h: h.tensor_sub(stat[:, 16:16 + NT], stat[:, 16:16 + NT], stat[:, 12:12 + NT]))
        t_eps_2 = OPt("ve", lambda h: h.tensor_scalar(stat[:, 16:16 + NT], stat[:, 16:16 + NT],
                                                      1e-5, None, OP.add), tag=True)
        tok_rstd = emit_rsqrt(stat[:, 16:16 + NT], stat[:, 20:20 + NT],
                              stat[:, 24:24 + NT], stat[:, 28:28 + NT], deps=[t_eps_2])
        t_h2T = []
        t_pt_free = None
        for i in range(NT):
            xi = x_sb[:, i * 1024:(i + 1) * 1024]
            OPt("ve", lambda h, a=p2scr[:, 0:1024], b=xi, m=stat[:, 8 + i:9 + i], r=stat[:, 20 + i:21 + i]:
                h.tensor_scalar(a, b, m, r, OP.subtract, OP.mult), deps=[tok_rstd, t_sq[i]])
            OPt("ve", lambda h, a=p2scr[:, 1024:2048], b=p2scr[:, 0:1024]:
                h.tensor_tensor(a, b, A2bc, OP.mult))
            t_h2b = OPt("ve", lambda h, a=p2scr[:, 0:1024], b=p2scr[:, 1024:2048]:
                        h.tensor_tensor(a, b, B2bc, OP.add), tag=True)
            t_tp = None
            for j in range(8):
                t_tp = OPt("te", lambda h, o=PT0[:, j * 128:(j + 1) * 128],
                           a=p2scr[:, j * 128:(j + 1) * 128], st=True:
                           h.matmul(o, a, cb["cI128"], start=st, stop=True, is_transpose=True),
                           deps=[t_h2b, t_pt_free, tok_w2], tag=(j == 7))
            t_pt_free = OPt("ve", lambda h,
                            o=h2T.rearrange("p (j t) -> p j t", j=8)[:, :, i * 128:(i + 1) * 128],
                            a=PT0.rearrange("p (j t) -> p j t", j=8):
                            h.tensor_copy(o, a), deps=[t_tp], tag=True)
            t_h2T.append(t_pt_free)
        tok_h2T = t_h2T[-1]
        t_spk = None
        t_ev2 = [None, None]
        for mo in range(16):
            t_mm = None
            for ko in range(8):
                t_mm = OPt("te", lambda h, o=Pb[mo % 2][:, 0:OC],
                           l=wenc_s[:, ko * 2048 + mo * 128: ko * 2048 + (mo + 1) * 128],
                           r=h2T[:, ko * OC:(ko + 1) * OC], st=(ko == 0), sp=(ko == 7):
                           h.matmul(o, l, r, start=st, stop=sp),
                           deps=[tok_h2T, t_ev2[mo % 2]], tag=(ko == 7))
            t_spk = OPt("se", lambda h, a=spk_s[:, mo * OC:(mo + 1) * OC], b=Pb[mo % 2][:, 0:OC],
                        bi=benc_h[:, mo:mo + 1]:
                        h.activation(a, b, AF.Tanh, bias=bi, scale=0.5),
                        deps=[t_mm], tag=True)
            t_ev2[mo % 2] = t_spk
        t_cm = None
        for ko in range(16):
            t_cm = OPt("te", lambda h, o=Pb[2][0:64, 0:OC], l=ws2c_s[:, ko * 64:(ko + 1) * 64],
                       r=spk_s[:, ko * OC:(ko + 1) * OC], st=(ko == 0):
                       h.matmul(o, l, r, start=st, stop=False),
                       deps=[t_ev2[0], t_ev2[1], tok_wpre], tag=(ko == 15))
        t_cm = OPt("te", lambda h: h.matmul(Pb[2][0:64, 0:OC], bs2cr[0:1, 0:64],
                                            cb["cONESROW"][0:1, 0:OC], start=False, stop=True),
                   deps=[tok_bs2cr], tag=True)
        t_ce = OPt("ve", lambda h: h.tensor_copy(contp[0:64, 0:OC], Pb[2][0:64, 0:OC]),
                   deps=[t_cm], tag=True)
        t_cr = OPt("te", lambda h: h.matmul(Pb[3][64:128, 0:OC], cb["cI128"][0:64, 0:64],
                                            contp[0:64, 0:OC], start=True, stop=True),
                   deps=[t_ce], tag=True)
        t_ce2 = OPt("ve", lambda h: h.tensor_copy(contp[64:128, 0:OC], Pb[3][64:128, 0:OC]),
                    deps=[t_cr], tag=True)
        t_lg = None
        for i in range(NT):
            OPt("te", lambda h, o=Pb[4][:, i * 8:(i + 1) * 8],
                l=contp[0:64, i * 128:(i + 1) * 128], r=wr_s[0:64, 0:8]:
                h.matmul(o, l, r, start=True, stop=False), deps=[t_ce, tok_w2, t_lg])
            t_lg = OPt("te", lambda h, o=Pb[4][:, i * 8:(i + 1) * 8],
                       l=cb["cONESROW"][0:1, 0:128], r=br_s[0:1, 0:8]:
                       h.matmul(o, l, r, start=False, stop=True), deps=[t_ce], tag=True)
        lv = Pb[4].rearrange("p (i e) -> p i e", e=8)[:, 0:NT, :]
        OPt("ve", lambda h, a=tk_f[:, 0:NT]: h.reduce_max(a, lv, AX.X), deps=[t_lg])
        t_sub = None
        for i in range(NT):
            t_sub = OPt("ve", lambda h, a=tk_f[:, 8 + i * 8:8 + (i + 1) * 8],
                        b=Pb[4][:, i * 8:(i + 1) * 8],
                        m=tk_f[:, i:i + 1]: h.tensor_scalar(a, b, m, None, OP.subtract),
                        tag=(i == NT - 1))
        t_ex = OPt("se", lambda h, a=tk_f[:, 8:8 + 8 * NT], b=tk_f[:, 8:8 + 8 * NT]:
                   h.activation(a, b, AF.Exp), deps=[t_sub], tag=True)
        OPt("ve", lambda h, a=gsel_f[:, 0:8 * NT], b=tk_f[:, 8:8 + 8 * NT]:
            h.tensor_scalar(a, b, 1.0, -1e30, OP.is_ge, OP.mult), deps=[t_ex])
        OPt("ve", lambda h, a=gsel_f[:, 0:8 * NT], b=gsel_f[:, 0:8 * NT]:
            h.tensor_tensor(a, b, tk_f[:, 8:8 + 8 * NT], OP.add))
        OPt("ve", lambda h, a=tk_f[:, 2:2 + NT],
            b=gsel_f[:, 0:8 * NT].rearrange("p (i e) -> p i e", e=8):
            h.reduce_max(a, b, AX.X))
        t_1m2 = OPt("ve", lambda h, a=tk_f[:, 4:4 + NT], b=tk_f[:, 2:2 + NT]:
                    h.tensor_scalar(a, b, 1.0, None, OP.add), tag=True)
        t_rln = OPt("se", lambda h, a=tk_f[:, 6:6 + NT], b=tk_f[:, 4:4 + NT]:
                    h.activation(a, b, AF.Ln), deps=[t_1m2], tag=True)
        t_rex = OPt("se", lambda h, a=tk_f[:, 6:6 + NT]:
                    h.activation(a, a, AF.Exp, scale=-1.0), tag=True)
        dep_r = [t_rex]
        for _ in range(2):
            OPt("ve", lambda h, a=stat[:, 32:32 + NT], b=tk_f[:, 6:6 + NT], c=tk_f[:, 4:4 + NT]:
                h.tensor_mul(a, b, c), deps=dep_r)
            dep_r = ()
            OPt("ve", lambda h, a=stat[:, 32:32 + NT]:
                h.tensor_scalar(a, a, -1.0, 2.0, OP.mult, OP.add))
            OPt("ve", lambda h, a=tk_f[:, 6:6 + NT], b=tk_f[:, 6:6 + NT], c=stat[:, 32:32 + NT]:
                h.tensor_mul(a, b, c))
        for i in range(NT):
            OPt("ve", lambda h, a=gsel_f[:, i * 8:(i + 1) * 8], b=tk_f[:, 8 + i * 8:8 + (i + 1) * 8],
                m=tk_f[:, 2 + i:3 + i]: h.tensor_scalar(a, b, m, None, OP.is_ge))
            OPt("ve", lambda h, a=gsel_f[:, i * 8:(i + 1) * 8], b=gsel_f[:, i * 8:(i + 1) * 8],
                c=tk_f[:, 8 + i * 8:8 + (i + 1) * 8]: h.tensor_tensor(a, b, c, OP.mult))
            OPt("ve", lambda h, a=gsel_f[:, i * 8:(i + 1) * 8], b=gsel_f[:, i * 8:(i + 1) * 8],
                m=tk_f[:, 6 + i:7 + i]: h.tensor_scalar(a, b, m, None, OP.mult))
        t_gselb = OPt("ve", lambda h: h.tensor_copy(gselb[:, 0:8 * NT], gsel_f[:, 0:8 * NT]), tag=True)
        t_gt = None
        t_gte = None
        for e in range(8):
            for i in range(NT):
                idx = e * NT + i
                t_gt = OPt("te", lambda h, o=PT0[0:1, (idx % 8) * 128:(idx % 8 + 1) * 128],
                           a=gselb[:, i * 8 + e:i * 8 + e + 1], st=True:
                           h.matmul(o, a, cb["cI128"], start=st, stop=True, is_transpose=True),
                           deps=[t_gselb, t_lg, t_gte], tag=(idx % 8 == 7))
                if idx % 8 == 7:
                    rnd = idx // 8
                    t_gte = OPt("ve", lambda h, a=gselT_s[0:1, rnd * 1024:(rnd + 1) * 1024],
                                b=PT0[0:1, 0:1024]: h.tensor_copy(a, b),
                                deps=[t_gt], tag=True)
        t_eo = None
        t_he = None
        for e in range(8):
            pbase = (e % 2) * 64
            t_gb = OPt("te", lambda h, o=Pb[4][0:64, 0:OC], l=cb["cONESROW"][0:1, 0:64],
                       r=gselT_s[0:1, e * NT * 128:(e + 1) * NT * 128]:
                       h.matmul(o, l, r, start=True, stop=True), deps=[t_gte, t_eo], tag=True)
            t_gbe = OPt("ve", lambda h: h.tensor_copy(gbc_s[0:64, 0:OC], Pb[4][0:64, 0:OC]),
                        deps=[t_gb], tag=True)
            for mo in range(8):
                t_hm = OPt("te", lambda h, o=Pb[5][:, 0:OC],
                           l=w1p_s[pbase:pbase + 64, (e // 2) * 1024 + mo * 128:(e // 2) * 1024 + (mo + 1) * 128],
                           r=contp[pbase:pbase + 64, 0:OC]:
                           h.matmul(o, l, r, start=True, stop=True),
                           deps=[t_ce2, tok_w2, t_he, t_eo], tag=True)
                t_he = OPt("se", lambda h, a=he_s[:, mo * OC:(mo + 1) * OC], b=Pb[5][:, 0:OC],
                           bi=b1p_s[:, e * 8 + mo:e * 8 + mo + 1]:
                           h.activation(a, b, AF.Relu, bias=bi), deps=[t_hm, t_b1p], tag=True)
            t_ym = None
            for ko in range(8):
                t_ym = OPt("te", lambda h, o=Pb[2][0:64, 0:OC],
                           l=w2p_s[:, (e * 8 + ko) * 64:(e * 8 + ko + 1) * 64],
                           r=he_s[:, ko * OC:(ko + 1) * OC], st=(ko == 0):
                           h.matmul(o, l, r, start=st, stop=False),
                           deps=[t_he, tok_w2, t_eo], tag=(ko == 7))
            t_ym = OPt("te", lambda h, l=b2r_s[0:1, e * 64:(e + 1) * 64]:
                       h.matmul(Pb[2][0:64, 0:OC], l, cb["cONESROW"][0:1, 0:OC],
                                start=False, stop=True),
                       deps=[t_ym], tag=True)
            OPt("ve", lambda h: h.tensor_tensor(yg_f[0:64, 0:OC], Pb[2][0:64, 0:OC],
                                                gbc_s[0:64, 0:OC], OP.mult), deps=[t_ym, t_gbe])
            if e == 0:
                t_eo = OPt("ve", lambda h: h.tensor_copy(eo_f[0:64, 0:OC], yg_f[0:64, 0:OC]), tag=True)
            else:
                t_eo = OPt("ve", lambda h: h.tensor_add(eo_f[0:64, 0:OC], eo_f[0:64, 0:OC],
                                                        yg_f[0:64, 0:OC]), tag=True)
        t_eob = OPt("ve", lambda h: h.tensor_copy(eobf[0:64, 0:OC], eo_f[0:64, 0:OC]),
                    deps=[t_eo], tag=True)
        t_ev3 = [None, None]
        for mo in range(16):
            t_mm = OPt("te", lambda h, o=Pb[mo % 2][:, 0:OC],
                       l=wc2s_s[0:64, mo * 128:(mo + 1) * 128], r=eobf[0:64, 0:OC]:
                       h.matmul(o, l, r, start=True, stop=True),
                       deps=[t_eob, tok_w2, t_ev3[mo % 2]], tag=True)
            t_ev3[mo % 2] = OPt("se", lambda h, a=smt_s[:, mo * OC:(mo + 1) * OC], b=Pb[mo % 2][:, 0:OC],
                                bi=bc2s_h[:, mo:mo + 1]:
                                h.activation(a, b, AF.Tanh, bias=bi, scale=0.5),
                                deps=[t_mm, t_bc2s], tag=True)
        t_res2 = None
        for i in range(NT):
            t_dm = None
            for nh in range(2):
                for ko in range(16):
                    t_dm = OPt("te", lambda h, o=Pb[nh][:, 0:512],
                               l=smt_s[:, ko * OC + i * 128: ko * OC + (i + 1) * 128],
                               r=wdec_s[:, ko * 1024 + nh * 512: ko * 1024 + (nh + 1) * 512],
                               st=(ko == 0): h.matmul(o, l, r, start=st, stop=False),
                               deps=[t_ev3[0], t_ev3[1], tok_wpre, t_res2], tag=False)
                t_dm = OPt("te", lambda h, o=Pb[nh][:, 0:512], l=cb["cONESROW"][0:1, 0:128],
                           r=bdecr[0:1, nh * 512:(nh + 1) * 512]:
                           h.matmul(o, l, r, start=False, stop=True), deps=[tok_bdec], tag=(nh == 1))
            t3a = OPt("se", lambda h, a=p2scr[:, 0:512], b=Pb[0][:, 0:512], acc=stat[:, 36:37]:
                      h.activation(a, b, AF.Square, accum_out=acc), deps=[t_dm], tag=True)
            t3b = OPt("se", lambda h, a=p2scr[:, 512:1024], b=Pb[1][:, 0:512], acc=stat[:, 37:38]:
                      h.activation(a, b, AF.Square, accum_out=acc), deps=[t_dm], tag=True)
            t3c = OPt("se", lambda h, a=dscr[:, 0:512], b=Pb[0][:, 0:512], acc=stat[:, 38:39]:
                      h.activation(a, b, AF.Copy, accum_out=acc), deps=[t_dm], tag=True)
            t3d = OPt("se", lambda h, a=dscr[:, 512:1024], b=Pb[1][:, 0:512], acc=stat[:, 39:40]:
                      h.activation(a, b, AF.Copy, accum_out=acc), deps=[t_dm], tag=True)
            OPt("ve", lambda h: h.tensor_add(stat[:, 38:39], stat[:, 38:39], stat[:, 39:40]),
                deps=[t3c, t3d])
            OPt("ve", lambda h: h.tensor_add(stat[:, 36:37], stat[:, 36:37], stat[:, 37:38]),
                deps=[t3a, t3b])
            OPt("ve", lambda h: h.tensor_scalar(stat[:, 38:39], stat[:, 38:39], INV_D, None, OP.mult))
            OPt("ve", lambda h: h.tensor_mul(stat[:, 40:41], stat[:, 38:39], stat[:, 38:39]))
            OPt("ve", lambda h: h.tensor_scalar(stat[:, 36:37], stat[:, 36:37], INV_D, None, OP.mult))
            OPt("ve", lambda h: h.tensor_sub(stat[:, 36:37], stat[:, 36:37], stat[:, 40:41]))
            t_eps_3 = OPt("ve", lambda h: h.tensor_scalar(stat[:, 36:37], stat[:, 36:37],
                                                          1e-5, None, OP.add), tag=True)
            emit_rsqrt(stat[:, 36:37], stat[:, 41:42], stat[:, 42:43], stat[:, 43:44],
                       deps=[t_eps_3])
            for nh in range(2):
                OPt("ve", lambda h, a=dscr[:, nh * 512:(nh + 1) * 512],
                    m=stat[:, 38:39], r=stat[:, 41:42]:
                    h.tensor_scalar(a, a, m, r, OP.subtract, OP.mult))
            OPt("ve", lambda h: h.tensor_tensor(dscr[:, 0:1024], dscr[:, 0:1024], G2bc, OP.mult),
                deps=[tok_G2])
            OPt("ve", lambda h: h.tensor_tensor(dscr[:, 0:1024], dscr[:, 0:1024], BB2bc, OP.add),
                deps=[tok_BB2])
            t_res2 = OPt("ve", lambda h, a=x_sb[:, i * 1024:(i + 1) * 1024]:
                         h.tensor_tensor(a, a, dscr[:, 0:1024], OP.add), tag=True)
        tok_res2 = t_res2
        tok_out = DMA("sy", lambda h, o=out_d[oc * OC:(oc + 1) * OC, :].rearrange(
            "(i p) f -> p i f", p=128): h.dma_start(o, x3), deps=[tok_res2])
        if dbg:
            dump("h2Td", h2T, oc, NOC, [tok_h2T])
            dump("spkd", spk_s, oc, NOC, [t_ev2[0], t_ev2[1]])
            dump("contd", contp, oc, NOC, [t_ce2])
            dump("gseld", gselb, oc, NOC, [t_gselb])
            dump("tkfd", tk_f[:, 0:24], oc, NOC, [t_gselb])
            dump("eod", eo_f[0:64, :], oc, NOC, [t_eob])
            dump("smtd", smt_s, oc, NOC, [t_ev3[0], t_ev3[1]])
    OPt("sy", lambda h: h.nop(), deps=[tok_out])

    global LAST_NRECS, LAST_RECS
    LAST_NRECS = len(sch.recs)
    LAST_RECS = sch.recs
    with nc.Block() as block:
        sch.replay(nc, block, sems, limit=limit)
    es.close()
    return nc


_NC_CACHE = {}
LAST_NRECS = 0
LAST_RECS = []

def kernel(**inputs):
    T = int(np.asarray(inputs["L0_attn_hidden"]).shape[1])
    if T not in _NC_CACHE:
        _NC_CACHE[T] = build_program(T=T)
    nc = _NC_CACHE[T]
    sh, per_core = host_prep(inputs)
    n = len(per_core)
    in_maps = []
    for i in range(n):
        m = dict(sh)
        m.update(per_core[i])
        in_maps.append(m)
    res = run_bass_kernel_spmd(nc, in_maps, core_ids=list(range(n)))
    out = np.stack([np.asarray(res.results[i]["out"]) for i in range(n)], axis=0)
    return out.astype(np.float32)
